# version 20 (final)
# baseline (speedup 1.0000x reference)
import numpy as np
import ml_dtypes
from contextlib import ExitStack

import concourse.bass as bass
import concourse.mybir as mybir
from concourse.bass_utils import run_bass_kernel_spmd

F32 = mybir.dt.float32
BF16 = mybir.dt.bfloat16
I32 = mybir.dt.int32
AF = mybir.ActivationFunctionType
ALU = mybir.AluOpType
AX = mybir.AxisListType

T = 8192
C = 1024
D = 64
TS = 2048
HID = 3072
QC = 512
NQC = T // QC
S127 = float(np.float32(1.0 / 127.0))
SCALE = 0.125
OWN_EN = True
TB_BATCH = True
DRAIN_NEW = True
NGRP = 16
TAIL_EN = True
B2_EN = True
SHUF_EN = True


class Sched:
    ENGINES = ("sync", "tensor", "vector", "scalar", "gpsimd")

    def __init__(self, nc, ctx):
        self.nc = nc
        self.ctx = ctx
        self.q = {e: [] for e in self.ENGINES}
        self.sems = {}
        self.cnt = {}
        for name in ("ds", "da", "dp", "pe", "dv", "ac", "cc"):
            self.mksem(name)

    def mksem(self, name):
        if name not in self.sems:
            self.sems[name] = self.ctx.enter_context(self.nc.semaphore(name))
            self.cnt[name] = 0
        return name

    def add(self, eng, fn, inc=(), wait=()):
        w = [(n, v) for (n, v) in wait if v > 0]
        for n, v in w:
            assert v <= self.cnt[n], f"wait on future: {n} {v} > {self.cnt[n]}"
        self.q[eng].append((w, fn, list(inc)))
        for n, v in inc:
            self.cnt[n] += v

    def dma(self, eng, out, in_, wait=(), sem=None):
        sem = self.mksem(sem) if sem else {"sync": "ds", "scalar": "da",
                                           "gpsimd": "dp"}[eng]
        self.add(eng, lambda q, out=out, in_=in_: q.dma_start(out=out, in_=in_),
                 inc=[(sem, 16)], wait=wait)
        return self.cnt[sem]

    def dmaT(self, eng, out, in_, wait=(), sem=None):
        sem = self.mksem(sem) if sem else {"sync": "ds", "scalar": "da"}[eng]
        self.add(eng, lambda q, out=out, in_=in_: q.dma_start_transpose(out, in_),
                 inc=[(sem, 16)], wait=wait)
        return self.cnt[sem]

    def emit(self, block):
        for eng in self.ENGINES:
            ops = self.q[eng]
            sems = self.sems

            def mk(ops):
                def go(q):
                    hi = {}
                    for w, fn, i in ops:
                        for n, v in w:
                            if v > hi.get(n, 0):
                                q.wait_ge(sems[n], v)
                                hi[n] = v
                        instr = fn(q)
                        for n, v in i:
                            instr.then_inc(sems[n], v)
                return go
            getattr(block, eng)(mk(ops))


def build_nc():
    nc = bass.Bass("TRN2", target_bir_lowering=False)

    x_full = nc.dram_tensor("x_full", [T, C], F32, kind="ExternalInput")
    x_own = nc.dram_tensor("x_own", [TS, C], F32, kind="ExternalInput")
    wqkv = nc.dram_tensor("wqkv", [C, 768], F32, kind="ExternalInput")
    bqkv = nc.dram_tensor("bqkv", [128, 6], F32, kind="ExternalInput")
    wproj = nc.dram_tensor("wproj", [256, C], F32, kind="ExternalInput")
    bproj = nc.dram_tensor("bproj", [128, 8], F32, kind="ExternalInput")
    wgT = nc.dram_tensor("wgT", [C, HID], I32, kind="ExternalInput")
    wuT = nc.dram_tensor("wuT", [C, HID], I32, kind="ExternalInput")
    wdT = nc.dram_tensor("wdT", [HID, C], I32, kind="ExternalInput")
    ident = nc.dram_tensor("ident", [128, 128], BF16, kind="ExternalInput")
    maskw = nc.dram_tensor("maskw", [128, 896], BF16, kind="ExternalInput")
    ones1 = nc.dram_tensor("ones1", [1, 128], F32, kind="ExternalInput")
    out_own = nc.dram_tensor("out_own", [TS, C], F32, kind="ExternalOutput")

    xnt_own_dram = nc.dram_tensor("xnt_own_dram", [128, 16384], BF16, kind="Internal")
    h_dram = nc.dram_tensor("h_dram", [HID, TS], BF16, kind="Internal")
    p_dram = nc.dram_tensor("p_dram", [C, TS], BF16, kind="Internal")
    wqkv_bf = nc.dram_tensor("wqkv_bf", [C, 768], BF16, kind="Internal")
    wproj_bf = nc.dram_tensor("wproj_bf", [256, C], BF16, kind="Internal")
    wg_bf = nc.dram_tensor("wg_bf", [C, HID], BF16, kind="Internal")
    wu_bf = nc.dram_tensor("wu_bf", [C, HID], BF16, kind="Internal")
    wd_bf = nc.dram_tensor("wd_bf", [HID, C], BF16, kind="Internal")
    wqkv_blk = nc.dram_tensor("wqkv_blk", [128, 6144], BF16, kind="Internal")
    wproj_blk = nc.dram_tensor("wproj_blk", [128, 2048], BF16, kind="Internal")
    wgu_blk = nc.dram_tensor("wgu_blk", [HID, 2048], BF16, kind="Internal")
    wd_blk = nc.dram_tensor("wd_blk", [128, 24576], BF16, kind="Internal")
    rs_in = nc.dram_tensor("rs_in", [4 * C, TS], F32, kind="Internal")
    rs_out = nc.dram_tensor("rs_out", [C, TS], F32, kind="Internal")

    BASE = 24576
    TOP = 229344
    allocs = []

    def sb(name, cols, dtype, off, ph0, ph1):
        size = cols * (2 if dtype in (BF16,) else 4)
        assert off % 32 == 0 and off >= BASE and off + size <= TOP, (name, off, size)
        for (n2, o2, s2, p0, p1) in allocs:
            if not (ph1 < p0 or p1 < ph0):
                assert off + size <= o2 or o2 + s2 <= off, f"overlap {name} vs {n2}"
        allocs.append((name, off, size, ph0, ph1))
        return nc.alloc_sbuf_tensor_at(name, [128, cols], dtype, offset=off)

    s_ident = sb("s_ident", 128, BF16, 24576, 0, 4)
    s_maskw = sb("s_maskw", 896, BF16, 24832, 0, 4)
    s_ones = nc.alloc_sbuf_tensor_at("s_ones", [1, 128], F32, offset=26624)
    allocs.append(("s_ones", 26624, 512, 0, 4))
    s_bqkv = sb("s_bqkv", 8, F32, 27136, 0, 4)
    s_bproj = sb("s_bproj", 8, F32, 27168, 0, 4)
    s_stats = sb("s_stats", 48, F32, 27200, 0, 4)

    s_QT = sb("s_QT", 16384, BF16, 27392, 0, 1)
    s_KT = sb("s_KT", 16384, BF16, 60160, 0, 1)
    s_VT = sb("s_VT", 16640, BF16, 92928, 0, 1)
    s_yroll = sb("s_yroll", 4096, BF16, 126208, 0, 2)
    s_xbuf = sb("s_xbuf", 4096, F32, 134400, 0, 1)
    s_xnst = sb("s_xnst", 4096, BF16, 150784, 0, 1)
    s_xntb = sb("s_xntb", 4096, BF16, 158976, 0, 1)
    s_wqkvb = sb("s_wqkvb", 6144, BF16, 167168, 0, 1)
    s_vstage = sb("s_vstage", 2048, BF16, 179456, 0, 1)
    s_ptb = sb("s_ptb", 4096, BF16, 183552, 0, 2)
    s_deninv = sb("s_deninv", 1024, F32, 191744, 0, 2)
    s_ycp = sb("s_ycp", 1024, F32, 195840, 0, 2)
    s_spstg = sb("s_spstg", 2048, F32, 199936, 0, 2)
    s_wprojb = sb("s_wprojb", 2048, BF16, 208128, 0, 2)
    s_ostg = sb("s_ostg", 4096, BF16, 212224, 0, 1)
    s_dbc = sb("s_dbc", 1024, F32, 220416, 0, 2)

    s_xnt_own = sb("s_xnt_own", 16384, BF16, 27392, 3, 4)
    s_wdb = sb("s_wdb", 24576, BF16, 64256, 3, 3)
    s_wgu = sb("s_wgu", 6144, BF16, 113408, 3, 3)
    s_sg = sb("s_sg", 2048, F32, 125696, 3, 3)
    s_hhh = sb("s_hhh", 4096, BF16, 133888, 3, 3)
    s_ht = sb("s_ht", 24576, BF16, 142080, 3, 3)
    s_gstg = sb("s_gstg", 1024, F32, 191232, 3, 3)

    s_s2 = sb("s_s2", 1024, F32, 195328, 3, 4)
    s_pt = sb("s_pt", 1024, BF16, 199424, 3, 4)
    s_ptT = sb("s_ptT", 2048, BF16, 201472, 3, 4)
    s_xo = sb("s_xo", 2048, F32, 205568, 3, 4)
    s_outt = sb("s_outt", 2048, F32, 213760, 3, 4)

    with ExitStack() as ctx:
        main = ctx.enter_context(nc.psum_tensor("main", [128, 3072], F32))
        ptA = ctx.enter_context(nc.psum_tensor("ptA", [128, 1024], BF16))
        ptB = ctx.enter_context(nc.psum_tensor("ptB", [128, 1024], BF16))
        s = Sched(nc, ctx)
        block = ctx.enter_context(nc.Block())

        s.dma("sync", s_ident[:], ident[:, :], sem="dcst")
        s.dma("sync", s_maskw[:], maskw[:, :], sem="dcst")
        s.dma("sync", s_ones[:], ones1[:, :], sem="dcst")
        s.dma("sync", s_bqkv[:, 0:6], bqkv[:, :], sem="dcst")
        s.dma("sync", s_bproj[:, 0:8], bproj[:, :], sem="dcst")
        ds_consts = ("dcst", s.cnt["dcst"])
        s.dma("gpsimd", wqkv_bf.ap()[:, :], wqkv.ap()[:, :], sem="dcast")
        s.dma("gpsimd", wproj_bf.ap()[:, :], wproj.ap()[:, :], sem="dcast")
        dp_cast0 = s.cnt["dcast"]
        if SHUF_EN:
          s.dma("gpsimd", wqkv_blk.ap().rearrange("p (a d) -> p a d", d=768),
              wqkv_bf.ap().rearrange("(a p) d -> p a d", p=128),
              wait=[("dcast", dp_cast0)], sem="dcast")
          s.dma("gpsimd", wproj_blk.ap().rearrange("p (a d) -> p a d", d=1024),
              wproj_bf.ap().rearrange("(a p) d -> p a d", p=128),
              wait=[("dcast", dp_cast0)], sem="dcast")
        dp_wcast = s.cnt["dcast"]
        s.dma("scalar", s_wqkvb[:], wqkv_blk.ap()[:, :],
              wait=[("dcast", dp_wcast)], sem="dwq")
        da_wqkv = ("dwq", s.cnt["dwq"])
        s.dma("scalar", s_wprojb[:], wproj_blk.ap()[:, :],
              wait=[("dcast", dp_wcast)], sem="dwp")
        da_wproj = ("dwp", s.cnt["dwp"])
        for half in range(2):
            s.dma("gpsimd", wg_bf.ap()[half * 512:(half + 1) * 512, :],
                  wgT.ap()[half * 512:(half + 1) * 512, :], sem="dcastp")
            s.dma("gpsimd", wu_bf.ap()[half * 512:(half + 1) * 512, :],
                  wuT.ap()[half * 512:(half + 1) * 512, :], sem="dcastp")
            s.dma("gpsimd", wd_bf.ap()[half * 1536:(half + 1) * 1536, :],
                  wdT.ap()[half * 1536:(half + 1) * 1536, :], sem="dcastp")
        dp_castp = s.cnt.get("dcastp", 0)
        for hh in range(24 if SHUF_EN else 0):
            r0, r1 = hh * 128, (hh + 1) * 128
            s.dma("gpsimd",
                  wgu_blk.ap()[r0:r1, 0:1024].rearrange("p (a h) -> p a h", h=128),
                  wg_bf.ap()[:, hh * 128:(hh + 1) * 128]
                  .rearrange("(a p) h -> p a h", p=128),
                  wait=[("dcastp", dp_castp)] if hh == 0 else [],
                  sem="dcastf")
            s.dma("gpsimd",
                  wgu_blk.ap()[r0:r1, 1024:2048].rearrange("p (a h) -> p a h", h=128),
                  wu_bf.ap()[:, hh * 128:(hh + 1) * 128]
                  .rearrange("(a p) h -> p a h", p=128),
                  sem="dcastf")
        for part in range(4 if SHUF_EN else 0):
            s.dma("gpsimd",
                  wd_blk.ap().rearrange("p (a c) -> p a c", c=1024)
                  [:, part * 6:(part + 1) * 6, :],
                  wd_bf.ap().rearrange("(a p) c -> p a c", p=128)
                  [:, part * 6:(part + 1) * 6, :],
                  sem="dcastf")
        dp_wbf = s.cnt.get("dcastf", 0)
        s.add("vector", lambda q: q.memset(s_VT[:], 1.0), inc=[("dv", 1)])
        dv_vtones = s.cnt["dv"]

        xcons_dv = [0, 0, 0, 0]
        xnst_pe = [0, 0, 0, 0]
        xnst_w_dv = [0, 0, 0, 0]
        xnst_sq_ac = [0, 0, 0, 0]
        x_loaded_ds = [0, 0, 0, 0]
        stats_free_dv = [0, 0, 0, 0]

        def ln_stage1(ti, src_rows):
            slot = ti % 4
            sb_ = slot * 10
            st = s_stats
            xsem = "dx%d" % slot
            x_loaded_ds[slot] = s.dma(
                "sync", s_xbuf[:, slot * 1024:(slot + 1) * 1024], src_rows,
                wait=[("dv", xcons_dv[slot]), ("ac", xnst_sq_ac[slot])], sem=xsem)
            xs = s_xbuf[:, slot * 1024:(slot + 1) * 1024]
            s.add("vector", lambda q, xs=xs, o=st[:, sb_:sb_ + 1]: q.reduce_sum(o, xs, axis=AX.X),
                  inc=[("dv", 1)],
                  wait=[(xsem, x_loaded_ds[slot]),
                        ("dv", max(stats_free_dv[slot], s.cnt["dv"]))])
            s.add("scalar", lambda q, xs=xs, o=st[:, sb_ + 1:sb_ + 2],
                  d_=s_xnst[:, slot * 1024:(slot + 1) * 1024]: q.activation(
                d_, xs, AF.Square, accum_out=o),
                inc=[("ac", 1)],
                wait=[(xsem, x_loaded_ds[slot]), ("dv", stats_free_dv[slot]),
                      ("ac", s.cnt["ac"]), ("pe", xnst_pe[slot]),
                      ("dv", xnst_w_dv[slot])])
            xnst_sq_ac[slot] = s.cnt["ac"]
            return s.cnt["dv"], s.cnt["ac"]

        def ln_stage2_group(tis, marks):
            def slots(ti):
                slot = ti % 4
                sb_ = slot * 10
                st = s_stats
                return slot, {
                    "s12": st[:, sb_ + 0:sb_ + 2], "mex": st[:, sb_ + 2:sb_ + 4],
                    "m": st[:, sb_ + 2:sb_ + 3], "ex2": st[:, sb_ + 3:sb_ + 4],
                    "nvar": st[:, sb_ + 5:sb_ + 6], "sd": st[:, sb_ + 6:sb_ + 7],
                    "rstd": st[:, sb_ + 7:sb_ + 8], "negb": st[:, sb_ + 8:sb_ + 9],
                    "echo": st[:, sb_ + 9:sb_ + 10],
                    "rn": st[:, sb_ + 7:sb_ + 9]}
            S_ = [slots(ti) for ti in tis]
            for (sl, a), (dv_red, ac_sq) in zip(S_, marks):
                s.add("scalar", lambda q, o=a["echo"], i_=s_stats[:, sl * 10 + 1:sl * 10 + 2]:
                      q.activation(o, i_, AF.Identity),
                      inc=[("ac", 1)], wait=[("ac", ac_sq)])
            ac_e = s.cnt["ac"]
            mex_marks = []
            for (sl, a), (dv_red, ac_sq) in zip(S_, marks):
                s.add("vector", lambda q, o=a["mex"], i_=a["s12"]: q.tensor_scalar_mul(
                    o, i_, 1.0 / 1024.0),
                    inc=[("dv", 1)], wait=[("ac", ac_e), ("dv", dv_red)])
                mex_marks.append(s.cnt["dv"])
            for (sl, a), mk in zip(S_, mex_marks):
                s.add("vector", lambda q, o=a["nvar"], i_=a["m"], e_=a["ex2"]:
                      q.scalar_tensor_tensor(o, i_, i_, e_, ALU.mult, ALU.subtract),
                      inc=[("dv", 1)], wait=[("dv", mk)])
            dv_nv = s.cnt["dv"]
            for (sl, a) in S_:
                s.add("vector", lambda q, o=a["echo"], i_=a["nvar"]: q.tensor_copy(o, i_),
                      inc=[("dv", 1)], wait=[("dv", dv_nv)])
            dv_nve = s.cnt["dv"]
            for (sl, a) in S_:
                s.add("scalar", lambda q, o=a["sd"], i_=a["nvar"]: q.activation(
                    o, i_, AF.Sqrt, scale=-1.0, bias=np.float32(1e-5)),
                    inc=[("ac", 1)], wait=[("dv", dv_nve), ("ac", s.cnt["ac"])])
            ac_sd = s.cnt["ac"]
            for (sl, a) in S_:
                s.add("scalar", lambda q, o=a["echo"], i_=a["sd"]: q.activation(
                    o, i_, AF.Identity),
                    inc=[("ac", 1)], wait=[("ac", ac_sd)])
            ac_sde = s.cnt["ac"]
            for (sl, a) in S_:
                s.add("vector", lambda q, o=a["rstd"], i_=a["sd"]: q.reciprocal(o, i_),
                      inc=[("dv", 1)], wait=[("ac", ac_sde), ("dv", s.cnt["dv"])])
            dv_rs = s.cnt["dv"]
            for (sl, a) in S_:
                s.add("vector", lambda q, o=a["negb"], i_=a["m"], r_=a["rstd"]:
                      q.scalar_tensor_tensor(o, i_, -1.0, r_, ALU.mult, ALU.mult),
                      inc=[("dv", 1)], wait=[("dv", dv_rs)])
            dv_nb = s.cnt["dv"]
            for (sl, a) in S_:
                s.add("vector", lambda q, o=a["echo"], i_=a["rn"]: q.reduce_sum(
                    o, i_, axis=AX.X),
                    inc=[("dv", 1)], wait=[("dv", max(dv_nb, s.cnt["dv"]))])
            dv_nbe = s.cnt["dv"]
            last = 0
            for (sl, a), ti in zip(S_, tis):
                xs = s_xbuf[:, sl * 1024:(sl + 1) * 1024]
                xo = s_xnst[:, sl * 1024:(sl + 1) * 1024]
                s.add("vector", lambda q, xs=xs, xo=xo, sc_=a["rstd"], b_=a["negb"]:
                      q.tensor_scalar(xo, xs, sc_, b_, ALU.mult, ALU.add),
                      inc=[("dv", 1)],
                      wait=[("dv", dv_nbe), ("pe", xnst_pe[sl]),
                            ("ac", xnst_sq_ac[sl])])
                xcons_dv[sl] = s.cnt["dv"]
                stats_free_dv[sl] = s.cnt["dv"]
                xnst_w_dv[sl] = s.cnt["dv"]
                last = s.cnt["dv"]
            return last

        def ln_group(tis, srcs):
            marks = [ln_stage1(ti, sr) for ti, sr in zip(tis, srcs)]
            return ln_stage2_group(tis, marks)

        tslot_dv = [dv_vtones, dv_vtones]
        tslot_idx = [0]

        def transpose_batch(srcs, evac_fn, waits, evac_tile=None):
            banks = (ptA, ptB)
            if TB_BATCH:
                sl = tslot_idx[0] % 2
                tslot_idx[0] += 1
                bank = banks[sl]
                for i, src in enumerate(srcs):
                    w = (list(waits) + [("dv", tslot_dv[sl]), ds_consts]) if i == 0 \
                        else []
                    s.add("tensor", lambda q, o=bank[:, i * 128:(i + 1) * 128],
                          i_=src: q.transpose(o, i_, s_ident[:]),
                          inc=[("pe", 1)], wait=w)
                pe_tr = s.cnt["pe"]
                evac_fn(bank[:, 0:512], pe_tr)
                tslot_dv[sl] = s.cnt["dv"]
            else:
                for i, src in enumerate(srcs):
                    sl = tslot_idx[0] % 2
                    tslot_idx[0] += 1
                    bank = banks[sl]
                    w = (list(waits) if i == 0 else []) + \
                        [("dv", tslot_dv[sl]), ds_consts]
                    s.add("tensor", lambda q, o=bank[:, 0:128],
                          i_=src: q.transpose(o, i_, s_ident[:]),
                          inc=[("pe", 1)], wait=w)
                    evac_tile(i, bank[:, 0:128], s.cnt["pe"])
                    tslot_dv[sl] = s.cnt["dv"]

        xntb_pe = [0]
        qkvps_dv = [0]
        vstage_dv = [0]
        s.mksem("dxt")

        def qkv_half(g, ac_ready):
            for ch in range(8):
                srcs = [s_xnst[:, i * 1024 + ch * 128: i * 1024 + (ch + 1) * 128]
                        for i in range(4)]

                def evac(slot_ap, pe_tr, ch=ch):
                    w = [("pe", pe_tr)]
                    if ch < 2:
                        w.append(("pe", xntb_pe[0]))
                    s.add("vector", lambda q, d=s_xntb[:, ch * 512:(ch + 1) * 512],
                          s2=slot_ap: q.tensor_copy(d, s2),
                          inc=[("dv", 1)], wait=w)

                def evac_t(i, tile_ap, pe_tr, ch=ch):
                    w = [("pe", pe_tr)]
                    if ch == 0 and i == 0:
                        w.append(("pe", xntb_pe[0]))
                    s.add("vector", lambda q,
                          d=s_xntb[:, ch * 512 + i * 128: ch * 512 + (i + 1) * 128],
                          s2=tile_ap: q.tensor_copy(d, s2),
                          inc=[("dv", 1)], wait=w)
                transpose_batch(srcs, evac, [("dv", ac_ready)], evac_t)
            da = s.cnt["dv"]
            for sl in range(4):
                xnst_pe[sl] = s.cnt["pe"]
            t0 = g * 512
            for dp_ in range(6):
                ps = main[:, 0:512]
                for ch in range(8):
                    w = [("dv", da), da_wqkv,
                         ("dv", max(qkvps_dv[0], b2_free_dv[0], last_bc[0])),
                         ("ac", last_exp_ac[0])] if ch == 0 else []
                    st_ap = s_wqkvb[:, ch * 768 + dp_ * 128: ch * 768 + (dp_ + 1) * 128]
                    rhs = s_xntb[:, ch * 512:(ch + 1) * 512]
                    s.add("tensor", lambda q, ps=ps, st_ap=st_ap, rhs=rhs, ch=ch:
                          q.matmul(ps, st_ap, rhs, start=(ch == 0), stop=(ch == 7)),
                          inc=[("pe", 1)], wait=w)
                pe_mm = s.cnt["pe"]
                xntb_pe[0] = pe_mm
                if dp_ < 2:
                    dst = s_QT[:, dp_ * 8192 + t0: dp_ * 8192 + t0 + 512]
                elif dp_ < 4:
                    dst = s_KT[:, (dp_ - 2) * 8192 + t0: (dp_ - 2) * 8192 + t0 + 512]
                else:
                    dst = s_vstage[:, (dp_ - 4) * 512:(dp_ - 4) * 512 + 512]
                bias_ap = s_bqkv[:, dp_:dp_ + 1]
                vw = [("pe", pe_mm)]
                if dp_ >= 4:
                    vw.append(("dv", vstage_dv[0]))
                s.add("vector", lambda q, dst=dst, ps=ps, b=bias_ap:
                      q.tensor_scalar(dst, ps, b, None, ALU.add),
                      inc=[("dv", 1)], wait=vw)
                qkvps_dv[0] = s.cnt["dv"]
                if dp_ >= 4:
                    pair = dp_ - 4
                    dv_evac = s.cnt["dv"]
                    srcs = [s_vstage[:, pair * 512 + i * 128: pair * 512 + (i + 1) * 128]
                            for i in range(4)]

                    def evacV(slot_ap, pe_tr, g=g, pair=pair):
                        for i in range(4):
                            kt = g * 4 + i
                            base = kt * 260 + 130 * pair
                            dst2 = s_VT[:, base:base + 130].rearrange(
                                "p (h x) -> p h x", x=65)[:, 0:2, 0:64]
                            src2 = slot_ap[:, i * 128:(i + 1) * 128].rearrange(
                                "p (h x) -> p h x", x=64)
                            s.add("vector", lambda q, d=dst2, s2=src2:
                                  q.tensor_copy(d, s2),
                                  inc=[("dv", 1)],
                                  wait=[("pe", pe_tr)] if i == 0 else [])

                    def evacV_t(i, tile_ap, pe_tr, g=g, pair=pair):
                        kt = g * 4 + i
                        base = kt * 260 + 130 * pair
                        dst2 = s_VT[:, base:base + 130].rearrange(
                            "p (h x) -> p h x", x=65)[:, 0:2, 0:64]
                        src2 = tile_ap.rearrange("p (h x) -> p h x", x=64)
                        s.add("vector", lambda q, d=dst2, s2=src2:
                              q.tensor_copy(d, s2),
                              inc=[("dv", 1)], wait=[("pe", pe_tr)])
                    transpose_batch(srcs, evacV, [("dv", dv_evac)], evacV_t)
                    vstage_dv[0] = s.cnt["dv"]

        ostg_dp = [0]
        dxt_cnt = [0]

        def own_chunk(oi):
            ac_own = ln_group([64 + oi * 4 + t for t in range(4)],
                              [x_own.ap()[(oi * 4 + t) * 128:(oi * 4 + t + 1) * 128, :]
                               for t in range(4)])
            for ch in range(8):
                srcs = [s_xnst[:, i * 1024 + ch * 128: i * 1024 + (ch + 1) * 128]
                        for i in range(4)]

                def evacO(slot_ap, pe_tr, ch=ch):
                    s.add("vector", lambda q, d=s_ostg[:, ch * 512:(ch + 1) * 512],
                          s2=slot_ap: q.tensor_copy(d, s2),
                          inc=[("dv", 1)],
                          wait=[("pe", pe_tr), ("dxt", ostg_dp[0])])

                def evacO_t(i, tile_ap, pe_tr, ch=ch):
                    s.add("vector", lambda q,
                          d=s_ostg[:, ch * 512 + i * 128: ch * 512 + (i + 1) * 128],
                          s2=tile_ap: q.tensor_copy(d, s2),
                          inc=[("dv", 1)],
                          wait=[("pe", pe_tr), ("dxt", ostg_dp[0])])
                transpose_batch(srcs, evacO, [("dv", ac_own)], evacO_t)
            for sl in range(4):
                xnst_pe[sl] = s.cnt["pe"]
            ostg_dp[0] = s.dma(
                "gpsimd",
                xnt_own_dram.ap().rearrange("p (ch t) -> p ch t", t=2048)
                [:, :, oi * 512:(oi + 1) * 512],
                s_ostg[:].rearrange("p (ch t) -> p ch t", t=512),
                wait=[("dv", s.cnt["dv"])], sem="dxt")
            dxt_cnt[0] = s.cnt["dxt"]

        grp_done_dv = {}
        last_exp_ac = [0, 0]
        last_av_pe = [0, 0, 0, 0]
        last_av_cons_dv = 0
        last_ymul_dv = 0
        b2_free_dv = [0]
        last_bc = [0]
        b2_yroll_pe = [0, 0, 0, 0]

        stb = [main[:, 0:1024], main[:, 1024:2048]]
        avps = main[:, 2048:3072]

        def emit_qk(qc, pair, kt):
            b_ = kt % 2
            w = [("ac", last_exp_ac[b_]), ("dv", grp_done_dv[qc])]
            if kt <= 1:
                w.append(("dv", max(b2_free_dv[0], last_bc[0])))
            stA = s_KT[0:64, pair * 8192 + kt * 128: pair * 8192 + (kt + 1) * 128]
            stB = s_KT[64:128, pair * 8192 + kt * 128: pair * 8192 + (kt + 1) * 128]
            rhA = s_QT[0:64, pair * 8192 + qc * 512: pair * 8192 + (qc + 1) * 512]
            rhB = s_QT[64:128, pair * 8192 + qc * 512: pair * 8192 + (qc + 1) * 512]
            s.add("tensor", lambda q, o=stb[b_][:, 0:512], a=stA, r=rhA:
                  q.matmul(o, a, r, start=True, stop=True),
                  inc=[("pe", 1)], wait=w)
            s.add("tensor", lambda q, o=stb[b_][:, 512:1024], a=stB, r=rhB:
                  q.matmul(o, a, r, start=True, stop=True, tile_position=(64, 0)),
                  inc=[("pe", 1)])
            return s.cnt["pe"]

        exp_info = {}

        def emit_exp(qc, pair, kt, pe_qk):
            b_ = kt % 2
            p4 = kt % 4
            pt = s_ptb[:, p4 * 1024:(p4 + 1) * 1024]
            s.add("scalar", lambda q, o=pt, i_=stb[b_][:, 0:1024]:
                  q.activation(o, i_, AF.Exp, scale=SCALE),
                  inc=[("ac", 1)],
                  wait=[("pe", max(pe_qk, last_av_pe[p4]))])
            last_exp_ac[b_] = s.cnt["ac"]
            ac_exp = s.cnt["ac"]
            dv_msk = 0
            if kt >= 4 * qc:
                o = kt * 128 - qc * 512
                msk = s_maskw[:, 384 - o: 896 - o]
                s.add("vector", lambda q, p0=pt[:, 0:512], m=msk:
                      q.tensor_mul(p0, p0, m), inc=[("dv", 1)],
                      wait=[("ac", ac_exp)])
                s.add("vector", lambda q, p1=pt[:, 512:1024], m=msk:
                      q.tensor_mul(p1, p1, m), inc=[("dv", 1)])
                dv_msk = s.cnt["dv"]
            exp_info[kt] = (ac_exp, dv_msk)

        def emit_av(qc, pair, kt, K):
            p4 = kt % 4
            pt = s_ptb[:, p4 * 1024:(p4 + 1) * 1024]
            ac_exp, dv_msk = exp_info[kt]
            avw = [("ac", ac_exp)]
            if dv_msk:
                avw.append(("dv", dv_msk))
            if kt == 0:
                avw.append(("dv", max(last_av_cons_dv, last_ymul_dv)))
            vA = s_VT[:, kt * 260 + 130 * pair: kt * 260 + 130 * pair + 65]
            vB = s_VT[:, kt * 260 + 130 * pair + 65: kt * 260 + 130 * pair + 130]
            s.add("tensor", lambda q, v=vA, p=pt[:, 0:512], st_=(kt == 0), sp=(kt == K - 1):
                  q.matmul(avps[0:65, 0:512], v, p, start=st_, stop=sp),
                  inc=[("pe", 1)], wait=avw)
            s.add("tensor", lambda q, v=vB, p=pt[:, 512:1024], st_=(kt == 0), sp=(kt == K - 1):
                  q.matmul(avps[0:65, 512:1024], v, p, start=st_, stop=sp),
                  inc=[("pe", 1)])
            last_av_pe[p4] = s.cnt["pe"]

        def attn_qc(qc):
            nonlocal last_av_cons_dv, last_ymul_dv
            for pair in range(2):
                K = 4 * qc + 4
                pe_qks = {}
                for kt in range(K):
                    pe_qks[kt] = emit_qk(qc, pair, kt)
                    emit_exp(qc, pair, kt, pe_qks[kt])
                    if kt >= 2:
                        emit_av(qc, pair, kt - 2, K)
                emit_av(qc, pair, K - 2, K)
                emit_av(qc, pair, K - 1, K)
                pe_av_done = s.cnt["pe"]
                s.add("vector", lambda q: q.tensor_copy(
                    s_ycp[0:64, 0:512], avps[0:64, 0:512]),
                    inc=[("dv", 1)], wait=[("pe", pe_av_done)])
                s.add("vector", lambda q: q.tensor_copy(
                    s_ycp[64:128, 0:512], avps[0:64, 512:1024]), inc=[("dv", 1)])
                s.add("vector", lambda q: q.tensor_copy(
                    s_deninv[0:1, 0:512], avps[64:65, 0:512]), inc=[("dv", 1)])
                s.add("vector", lambda q: q.tensor_copy(
                    s_deninv[0:1, 512:1024], avps[64:65, 512:1024]),
                    inc=[("dv", 1)])
                dv_den = s.cnt["dv"]
                last_av_cons_dv = dv_den
                s.add("tensor", lambda q: q.matmul(
                    avps[:, 0:512], s_ones[0:1, :], s_deninv[0:1, 0:512],
                    start=True, stop=True),
                    inc=[("pe", 1)], wait=[("dv", dv_den)])
                s.add("tensor", lambda q: q.matmul(
                    avps[:, 512:1024], s_ones[0:1, :], s_deninv[0:1, 512:1024],
                    start=True, stop=True), inc=[("pe", 1)])
                pe_bc = s.cnt["pe"]
                s.add("vector", lambda q: q.reciprocal(
                    s_dbc[:, 0:512], avps[:, 0:512]),
                    inc=[("dv", 1)], wait=[("pe", pe_bc), ("dv", s.cnt["dv"])])
                s.add("vector", lambda q: q.reciprocal(
                    s_dbc[:, 512:1024], avps[:, 512:1024]),
                    inc=[("dv", 1)], wait=[("dv", s.cnt["dv"])])
                dv_rec = s.cnt["dv"]
                last_av_cons_dv = dv_rec
                sl4 = qc % 4
                yb = sl4 * 1024 + pair * 512
                yslA = s_yroll[0:64, yb:yb + 512]
                yslB = s_yroll[64:128, yb:yb + 512]
                s.add("vector", lambda q, o=yslA, i0=s_ycp[0:64, 0:512],
                      i1=s_dbc[0:64, 0:512]: q.tensor_mul(o, i0, i1),
                      inc=[("dv", 1)],
                      wait=[("dv", dv_rec), ("pe", b2_yroll_pe[sl4])])
                s.add("vector", lambda q, o=yslB, i0=s_ycp[64:128, 0:512],
                      i1=s_dbc[64:128, 512:1024]: q.tensor_mul(o, i0, i1),
                      inc=[("dv", 1)])
                last_ymul_dv = s.cnt["dv"]

        b2_dv = [0, 0, 0, 0]
        b2_ds = [0, 0, 0, 0]
        for _k in range(4):
            s.mksem("drs%d" % _k)

        def b2_chunk(g):
            sl4 = g % 4
            for cp in range(8):
                k = cp % 4
                ps = main[:, k * 512:(k + 1) * 512]
                for pr in range(2):
                    w = []
                    if pr == 0:
                        w = [("dv", max(b2_dv[k], last_ymul_dv)), da_wproj,
                             ("ac", max(last_exp_ac[0], last_exp_ac[1]))]
                    st_ap = s_wprojb[:, pr * 1024 + cp * 128: pr * 1024 + (cp + 1) * 128]
                    rhs = s_yroll[:, sl4 * 1024 + pr * 512: sl4 * 1024 + (pr + 1) * 512]
                    s.add("tensor", lambda q, ps=ps, a=st_ap, r=rhs, pr=pr:
                          q.matmul(ps, a, r, start=(pr == 0), stop=(pr == 1)),
                          inc=[("pe", 1)], wait=w)
                pe_mm = s.cnt["pe"]
                stg = s_spstg[:, k * 512:(k + 1) * 512]
                s.add("vector", lambda q, o=stg, i_=ps: q.tensor_copy(o, i_),
                      inc=[("dv", 1)], wait=[("pe", pe_mm), ("drs%d" % k, b2_ds[k])])
                b2_dv[k] = s.cnt["dv"]
                rb = 1024 * (g // 4) + cp * 128
                cb = (g % 4) * 512
                b2_ds[k] = s.dma("sync", rs_in.ap()[rb:rb + 128, cb:cb + 512], stg,
                                 wait=[("dv", b2_dv[k])], sem="drs%d" % k)
            b2_yroll_pe[sl4] = s.cnt["pe"]
            b2_free_dv[0] = max(b2_dv)

        OWN_AT = {8: 0, 10: 1, 12: 2, 14: 3}
        ac_ready = ln_group([t for t in range(4)],
                            [x_full.ap()[ti * 128:(ti + 1) * 128, :]
                             for ti in range(4)])
        for g in range(NGRP):
            qkv_half(g, ac_ready)
            grp_done_dv[g] = s.cnt["dv"]
            if OWN_EN and g in OWN_AT:
                own_chunk(OWN_AT[g])
            if g < 15:
                tis = [(g + 1) * 4 + t for t in range(4)]
                ac_ready = ln_group(
                    tis, [x_full.ap()[ti * 128:(ti + 1) * 128, :] for ti in tis])
            attn_qc(g)
            if B2_EN:
                b2_chunk(g)
        da_xnt_own = dxt_cnt[0]
        ds_rs_in = [("drs%d" % _k, s.cnt["drs%d" % _k]) for _k in range(4)]
        pe_B_end = s.cnt["pe"]
        dv_b2_end = s.cnt["dv"]
        if not TAIL_EN:
            final_waits = [(n, s.cnt[n]) for n in list(s.cnt.keys())
                           if n not in ("pe", "dv", "ac", "cc") and s.cnt[n] > 0]
            s.add("sync", lambda q: q.nop(), wait=final_waits)
            s.add("sync", lambda q: q.dma_start(
                out=out_own.ap()[0:128, :], in_=s_xbuf[:, 0:1024]),
                inc=[("ds", 16)])
            s.add("sync", lambda q: q.nop(), wait=[("ds", s.cnt["ds"])])
            s.emit(block)
            return nc

        wgu_da = []
        for hh in range(2):
            wsem = "dwg%d" % (hh % 3)
            s.dma("scalar", s_wgu[:, (hh % 3) * 2048:(hh % 3) * 2048 + 1024],
                  wgu_blk.ap()[hh * 128:(hh + 1) * 128, 0:1024],
                  wait=[("pe", pe_B_end), ("dcastf", dp_wbf)], sem=wsem)
            s.dma("gpsimd", s_wgu[:, (hh % 3) * 2048 + 1024:(hh % 3) * 2048 + 2048],
                  wgu_blk.ap()[hh * 128:(hh + 1) * 128, 1024:2048],
                  wait=[("pe", pe_B_end), ("dcastf", dp_wbf)], sem=wsem)
            wgu_da.append((wsem, s.cnt[wsem]))
        s.add("gpsimd", lambda q: q.collective_compute(
            "ReduceScatter", ALU.add,
            replica_groups=[[0, 1, 2, 3], [4, 5, 6, 7]],
            ins=[rs_in.ap().opt()], outs=[rs_out.ap().opt()]),
            inc=[("cc", 1)], wait=ds_rs_in)
        for part in range(4):
            s.dma("sync", s_wdb[:, part * 6144:(part + 1) * 6144],
                  wd_blk.ap()[:, part * 6144:(part + 1) * 6144],
                  wait=[("pe", pe_B_end), ("dcastf", dp_wbf)], sem="dwd")
        da_wd = ("dwd", s.cnt["dwd"])

        s.dma("gpsimd", s_xnt_own[:], xnt_own_dram.ap()[:, :],
              wait=[("dxt", da_xnt_own), ("pe", pe_B_end)], sem="dxc")
        da_xnt_own_c = s.cnt["dxc"]
        gate_bank_ac = [0, 0]
        up_bank_dv = [0, 0]
        sg_dv = [0, 0, 0, 0]
        hhh_ds = [0, 0]
        gu_pe = [0, 0, 0]
        for hh in range(24):
            if hh >= 2:
                wsem = "dwg%d" % (hh % 3)
                s.dma("scalar", s_wgu[:, (hh % 3) * 2048:(hh % 3) * 2048 + 1024],
                      wgu_blk.ap()[hh * 128:(hh + 1) * 128, 0:1024],
                      wait=[("pe", gu_pe[hh % 3])], sem=wsem)
                s.dma("gpsimd", s_wgu[:, (hh % 3) * 2048 + 1024:(hh % 3) * 2048 + 2048],
                      wgu_blk.ap()[hh * 128:(hh + 1) * 128, 1024:2048],
                      wait=[("pe", gu_pe[hh % 3])], sem=wsem)
                wgu_da.append((wsem, s.cnt[wsem]))
            hb = hh % 2
            dv_last_stt = 0
            for tc in range(4):
                tb = tc % 2
                pg = main[:, tb * 512:(tb + 1) * 512]
                pu = main[:, 1024 + tb * 512: 1024 + (tb + 1) * 512]
                for ch in range(8):
                    w = [wgu_da[hh], ("dxc", da_xnt_own_c), ("ac", gate_bank_ac[tb]),
                         ("dv", dv_b2_end)] if ch == 0 else []
                    st_ap = s_wgu[:, (hh % 3) * 2048 + ch * 128:(hh % 3) * 2048 + (ch + 1) * 128]
                    rhs = s_xnt_own[:, ch * 2048 + tc * 512: ch * 2048 + (tc + 1) * 512]
                    s.add("tensor", lambda q, ps=pg, a=st_ap, r=rhs, ch=ch:
                          q.matmul(ps, a, r, start=(ch == 0), stop=(ch == 7)),
                          inc=[("pe", 1)], wait=w)
                pe_g = s.cnt["pe"]
                for ch in range(8):
                    w = [("dv", up_bank_dv[tb])] if ch == 0 else []
                    st_ap = s_wgu[:, (hh % 3) * 2048 + 1024 + ch * 128:
                                  (hh % 3) * 2048 + 1024 + (ch + 1) * 128]
                    rhs = s_xnt_own[:, ch * 2048 + tc * 512: ch * 2048 + (tc + 1) * 512]
                    s.add("tensor", lambda q, ps=pu, a=st_ap, r=rhs, ch=ch:
                          q.matmul(ps, a, r, start=(ch == 0), stop=(ch == 7)),
                          inc=[("pe", 1)], wait=w)
                pe_u = s.cnt["pe"]
                gu_pe[hh % 3] = pe_u
                sgs = s_sg[:, tc * 512:(tc + 1) * 512]
                s.add("scalar", lambda q, o=sgs, i_=pg: q.activation(
                    o, i_, AF.Silu, scale=S127),
                    inc=[("ac", 1)], wait=[("pe", pe_g), ("dv", sg_dv[tc])])
                gate_bank_ac[tb] = s.cnt["ac"]
                ac_sg = s.cnt["ac"]
                hsl = s_hhh[:, hb * 2048 + tc * 512: hb * 2048 + (tc + 1) * 512]
                s.add("vector", lambda q, o=hsl, i_=pu, g_=sgs: q.scalar_tensor_tensor(
                    o, i_, S127, g_, ALU.mult, ALU.mult),
                    inc=[("dv", 1)],
                    wait=[("ac", ac_sg), ("pe", pe_u), ("dh%d" % hb, hhh_ds[hb])])
                up_bank_dv[tb] = s.cnt["dv"]
                sg_dv[tc] = s.cnt["dv"]
                dv_last_stt = s.cnt["dv"]
            hhh_ds[hb] = s.dma("sync", h_dram.ap()[hh * 128:(hh + 1) * 128, :],
                               s_hhh[:, hb * 2048:(hb + 1) * 2048],
                               wait=[("dv", dv_last_stt)], sem="dh%d" % hb)

        hh_store = [s.cnt.get("dh0", 0), s.cnt.get("dh1", 0)]
        ht_ds = [0, 0]
        ht_pe = [0, 0]
        dps_ac = [0, 0]
        gstg_dv = [0, 0]
        s2_ds = [0, 0]
        s2_dv = [0, 0]
        p_dv = [("dpp0", 0), ("dpp1", 0)]
        out_dv = [0, 0]
        out_st = [0, 0]
        dpt_mark = [0, 0]
        for tc in range(4):
            tb = tc % 2
            psem = "dpp%d" % tb
            s.dma("scalar",
                  s_ht[:, tb * 12288:tb * 12288 + 6144]
                  .rearrange("p (a t) -> p a t", t=512),
                  h_dram.ap()[0:1536, tc * 512:(tc + 1) * 512]
                  .rearrange("(a p) t -> p a t", p=128),
                  wait=[("dh0", hh_store[0]), ("dh1", hh_store[1]),
                        ("pe", ht_pe[tb])],
                  sem="dht%d" % tb)
            ht_ds[tb] = s.dma("gpsimd",
                              s_ht[:, tb * 12288 + 6144:(tb + 1) * 12288]
                              .rearrange("p (a t) -> p a t", t=512),
                              h_dram.ap()[1536:3072, tc * 512:(tc + 1) * 512]
                              .rearrange("(a p) t -> p a t", p=128),
                              wait=[("dh0", hh_store[0]), ("dh1", hh_store[1]),
                                    ("pe", ht_pe[tb])],
                              sem="dht%d" % tb)
            for cp in range(8):
                k = (tc * 8 + cp) % 2
                s2_ds[k] = s.dma("gpsimd", s_s2[:, k * 512:(k + 1) * 512],
                                 rs_out.ap()[cp * 128:(cp + 1) * 128,
                                             tc * 512:(tc + 1) * 512],
                                 wait=[("cc", 1), ("dv", s2_dv[k])],
                                 sem="ds2%d" % k)
                pd = main[:, 2048 + k * 512: 2048 + (k + 1) * 512]
                for a in range(24):
                    w = [("dht%d" % tb, ht_ds[tb]), da_wd,
                         ("ac", dps_ac[k])] if a == 0 else []
                    if a == 0 and cp == 0 and tc == 0:
                        w.append(("dv", last_ymul_dv))
                    if a == 0 and cp == 0:
                        w.append(("dpt0", dpt_mark[0]))
                        w.append(("dpt1", dpt_mark[1]))
                    st_ap = s_wdb[:, a * 1024 + cp * 128: a * 1024 + (cp + 1) * 128]
                    rhs = s_ht[:, tb * 12288 + a * 512: tb * 12288 + (a + 1) * 512]
                    s.add("tensor", lambda q, ps=pd, a_=st_ap, r=rhs, a2=a:
                          q.matmul(ps, a_, r, start=(a2 == 0), stop=(a2 == 23)),
                          inc=[("pe", 1)], wait=w)
                pe_d = s.cnt["pe"]
                ht_pe[tb] = pe_d
                gst = s_gstg[:, k * 512:(k + 1) * 512]
                s.add("scalar", lambda q, o=gst, i_=pd: q.activation(
                    o, i_, AF.Silu, scale=S127),
                    inc=[("ac", 1)], wait=[("pe", pe_d), ("dv", gstg_dv[k])])
                dps_ac[k] = s.cnt["ac"]
                ptl = s_pt[:, k * 512:(k + 1) * 512]
                s.add("vector", lambda q, o=ptl, s_=s_s2[:, k * 512:(k + 1) * 512],
                      g_=gst, b=s_bproj[:, cp:cp + 1]:
                      q.scalar_tensor_tensor(o, s_, b, g_, ALU.add, ALU.mult),
                      inc=[("dv", 1)],
                      wait=[("ac", dps_ac[k]), ("ds2%d" % k, s2_ds[k]), p_dv[k]])
                gstg_dv[k] = s.cnt["dv"]
                s2_dv[k] = s.cnt["dv"]
                dv_p = s.cnt["dv"]
                s.dma("sync", p_dram.ap()[cp * 128:(cp + 1) * 128,
                                          tc * 512:(tc + 1) * 512],
                      ptl, wait=[("dv", dv_p)], sem=psem)
                p_dv[k] = (psem, s.cnt[psem])
            tc_p_ds = s.cnt[psem]
            for i in range(4):
                tt = tc * 4 + i
                kk = tt % 2
                tsem = "dpt%d" % kk
                s.dmaT("scalar", s_ptT[:, kk * 1024:(kk + 1) * 1024],
                       p_dram.ap()[:, tt * 128:(tt + 1) * 128],
                       wait=[(psem, tc_p_ds), ("dv", out_dv[kk])], sem=tsem)
                da_p = s.cnt[tsem]
                dpt_mark[kk] = da_p
                ds_x = s.dma("sync", s_xo[:, kk * 1024:(kk + 1) * 1024],
                             x_own.ap()[tt * 128:(tt + 1) * 128, :],
                             wait=[("dv", out_dv[kk])], sem="dxo%d" % kk)
                s.add("vector", lambda q, o=s_outt[:, kk * 1024:(kk + 1) * 1024],
                      a=s_xo[:, kk * 1024:(kk + 1) * 1024],
                      b=s_ptT[:, kk * 1024:(kk + 1) * 1024]:
                      q.tensor_add(o, a, b),
                      inc=[("dv", 1)],
                      wait=[(tsem, da_p), ("dxo%d" % kk, ds_x),
                            ("dout%d" % kk, out_st[kk])])
                out_dv[kk] = s.cnt["dv"]
                out_st[kk] = s.dma("sync", out_own.ap()[tt * 128:(tt + 1) * 128, :],
                                   s_outt[:, kk * 1024:(kk + 1) * 1024],
                                   wait=[("dv", out_dv[kk])], sem="dout%d" % kk)
        final_waits = [(n, s.cnt[n]) for n in list(s.cnt.keys())
                       if n not in ("pe", "dv", "ac", "cc") and s.cnt[n] > 0]
        s.add("sync", lambda q: q.nop(), wait=final_waits)
        s.emit(block)
    return nc


_NC_CACHE = {}


def kernel(L0_attn_x, L0_attn_W_attn, L0_attn_b_attn, L0_attn_W_proj, L0_attn_b_proj,
           L0_ffn_w_gate, L0_ffn_w_up, L0_ffn_w_down):
    x = np.asarray(L0_attn_x)
    W_attn = np.asarray(L0_attn_W_attn)
    b_attn = np.asarray(L0_attn_b_attn)
    W_proj = np.asarray(L0_attn_W_proj)
    b_proj = np.asarray(L0_attn_b_proj)
    wgT = np.ascontiguousarray(np.asarray(L0_ffn_w_gate).T)
    wuT = np.ascontiguousarray(np.asarray(L0_ffn_w_up).T)
    wdT = np.ascontiguousarray(np.asarray(L0_ffn_w_down).T)

    identc = np.eye(128, dtype=np.float32).astype(ml_dtypes.bfloat16)
    mw = np.zeros((128, 896), dtype=np.float32)
    r = np.arange(128)[:, None]
    cc = np.arange(896)[None, :]
    mw[(r <= cc - 384)] = 1.0
    maskwc = mw.astype(ml_dtypes.bfloat16)
    onesc = np.ones((1, 128), dtype=np.float32)
    bproj_t = np.ascontiguousarray(b_proj.reshape(8, 128).T.astype(np.float32))

    if "nc" not in _NC_CACHE:
        _NC_CACHE["nc"] = build_nc()
    nc = _NC_CACHE["nc"]

    in_maps = []
    for core in range(8):
        b, j = core // 4, core % 4
        qs = W_attn[:, 256 * j:256 * (j + 1)]
        ks = W_attn[:, 1024 + 256 * j:1024 + 256 * (j + 1)]
        vs = W_attn[:, 2048 + 256 * j:2048 + 256 * (j + 1)]
        wqkv = np.ascontiguousarray(np.concatenate([qs, ks, vs], axis=1))
        bq = np.concatenate([b_attn[256 * j:256 * (j + 1)],
                             b_attn[1024 + 256 * j:1024 + 256 * (j + 1)],
                             b_attn[2048 + 256 * j:2048 + 256 * (j + 1)]])
        bqkv_t = np.ascontiguousarray(bq.reshape(6, 128).T.astype(np.float32))
        in_maps.append({
            "x_full": np.ascontiguousarray(x[b]),
            "x_own": np.ascontiguousarray(x[b, j * TS:(j + 1) * TS]),
            "wqkv": wqkv,
            "bqkv": bqkv_t,
            "wproj": np.ascontiguousarray(W_proj[256 * j:256 * (j + 1), :]),
            "bproj": bproj_t,
            "wgT": wgT, "wuT": wuT, "wdT": wdT,
            "ident": identc, "maskw": maskwc, "ones1": onesc,
        })
    res = run_bass_kernel_spmd(nc, in_maps, core_ids=list(range(8)))
    out = np.empty((2, T, C), dtype=np.float32)
    for core in range(8):
        b, j = core // 4, core % 4
        out[b, j * TS:(j + 1) * TS] = res.results[core]["out_own"]
    return out


# version 22
# speedup vs baseline: 1.0996x; 1.0996x over previous
import numpy as np
import ml_dtypes
from contextlib import ExitStack

import concourse.bass as bass
import concourse.mybir as mybir
from concourse.bass_utils import run_bass_kernel_spmd

F32 = mybir.dt.float32
BF16 = mybir.dt.bfloat16
I32 = mybir.dt.int32
AF = mybir.ActivationFunctionType
ALU = mybir.AluOpType
AX = mybir.AxisListType

T = 8192
C = 1024
D = 64
TS = 2048
HID = 3072
QC = 512
NQC = T // QC
S127 = float(np.float32(1.0 / 127.0))
SCALE = 0.125
OWN_EN = True
TB_BATCH = True
DRAIN_NEW = True
NGRP = 16
TAIL_EN = True
B2_EN = True
SHUF_EN = True


class Sched:
    ENGINES = ("sync", "tensor", "vector", "scalar", "gpsimd")

    def __init__(self, nc, ctx):
        self.nc = nc
        self.ctx = ctx
        self.q = {e: [] for e in self.ENGINES}
        self.sems = {}
        self.cnt = {}
        for name in ("ds", "da", "dp", "pe", "dv", "ac", "cc"):
            self.mksem(name)

    def mksem(self, name):
        if name not in self.sems:
            self.sems[name] = self.ctx.enter_context(self.nc.semaphore(name))
            self.cnt[name] = 0
        return name

    def add(self, eng, fn, inc=(), wait=()):
        w = [(n, v) for (n, v) in wait if v > 0]
        for n, v in w:
            assert v <= self.cnt[n], f"wait on future: {n} {v} > {self.cnt[n]}"
        self.q[eng].append((w, fn, list(inc)))
        for n, v in inc:
            self.cnt[n] += v

    def dma(self, eng, out, in_, wait=(), sem=None):
        sem = self.mksem(sem) if sem else {"sync": "ds", "scalar": "da",
                                           "gpsimd": "dp"}[eng]
        self.add(eng, lambda q, out=out, in_=in_: q.dma_start(out=out, in_=in_),
                 inc=[(sem, 16)], wait=wait)
        return self.cnt[sem]

    def dmaT(self, eng, out, in_, wait=(), sem=None):
        sem = self.mksem(sem) if sem else {"sync": "ds", "scalar": "da"}[eng]
        self.add(eng, lambda q, out=out, in_=in_: q.dma_start_transpose(out, in_),
                 inc=[(sem, 16)], wait=wait)
        return self.cnt[sem]

    def emit(self, block):
        for eng in self.ENGINES:
            ops = self.q[eng]
            sems = self.sems

            def mk(ops):
                def go(q):
                    hi = {}
                    for w, fn, i in ops:
                        for n, v in w:
                            if v > hi.get(n, 0):
                                q.wait_ge(sems[n], v)
                                hi[n] = v
                        instr = fn(q)
                        for n, v in i:
                            instr.then_inc(sems[n], v)
                return go
            getattr(block, eng)(mk(ops))


def build_nc():
    nc = bass.Bass("TRN2", target_bir_lowering=False)

    x_full = nc.dram_tensor("x_full", [T, C], F32, kind="ExternalInput")
    x_own = nc.dram_tensor("x_own", [TS, C], F32, kind="ExternalInput")
    wqkv = nc.dram_tensor("wqkv", [C, 768], F32, kind="ExternalInput")
    bqkv = nc.dram_tensor("bqkv", [128, 6], F32, kind="ExternalInput")
    wproj = nc.dram_tensor("wproj", [256, C], F32, kind="ExternalInput")
    bproj = nc.dram_tensor("bproj", [128, 8], F32, kind="ExternalInput")
    wgT = nc.dram_tensor("wgT", [C, HID], I32, kind="ExternalInput")
    wuT = nc.dram_tensor("wuT", [C, HID], I32, kind="ExternalInput")
    wdT = nc.dram_tensor("wdT", [HID, C], I32, kind="ExternalInput")
    ident = nc.dram_tensor("ident", [128, 128], BF16, kind="ExternalInput")
    maskw = nc.dram_tensor("maskw", [128, 896], BF16, kind="ExternalInput")
    ones1 = nc.dram_tensor("ones1", [1, 128], F32, kind="ExternalInput")
    out_own = nc.dram_tensor("out_own", [TS, C], F32, kind="ExternalOutput")

    xnt_own_dram = nc.dram_tensor("xnt_own_dram", [128, 16384], BF16, kind="Internal")
    h_dram = nc.dram_tensor("h_dram", [HID, TS], BF16, kind="Internal")
    p_dram = nc.dram_tensor("p_dram", [C, TS], BF16, kind="Internal")
    wqkv_bf = nc.dram_tensor("wqkv_bf", [C, 768], BF16, kind="Internal")
    wproj_bf = nc.dram_tensor("wproj_bf", [256, C], BF16, kind="Internal")
    wg_bf = nc.dram_tensor("wg_bf", [C, HID], BF16, kind="Internal")
    wu_bf = nc.dram_tensor("wu_bf", [C, HID], BF16, kind="Internal")
    wd_bf = nc.dram_tensor("wd_bf", [HID, C], BF16, kind="Internal")
    wqkv_blk = nc.dram_tensor("wqkv_blk", [128, 6144], BF16, kind="Internal")
    wproj_blk = nc.dram_tensor("wproj_blk", [128, 2048], BF16, kind="Internal")
    wgu_blk = nc.dram_tensor("wgu_blk", [HID, 2048], BF16, kind="Internal")
    wd_blk = nc.dram_tensor("wd_blk", [128, 24576], BF16, kind="Internal")
    rs_in = nc.dram_tensor("rs_in", [4 * C, TS], F32, kind="Internal")
    rs_out = nc.dram_tensor("rs_out", [C, TS], F32, kind="Internal")

    BASE = 24576
    TOP = 229344
    allocs = []

    def sb(name, cols, dtype, off, ph0, ph1):
        size = cols * (2 if dtype in (BF16,) else 4)
        assert off % 32 == 0 and off >= BASE and off + size <= TOP, (name, off, size)
        for (n2, o2, s2, p0, p1) in allocs:
            if not (ph1 < p0 or p1 < ph0):
                assert off + size <= o2 or o2 + s2 <= off, f"overlap {name} vs {n2}"
        allocs.append((name, off, size, ph0, ph1))
        return nc.alloc_sbuf_tensor_at(name, [128, cols], dtype, offset=off)

    s_ident = sb("s_ident", 128, BF16, 24576, 0, 4)
    s_maskw = sb("s_maskw", 896, BF16, 24832, 0, 4)
    s_ones = nc.alloc_sbuf_tensor_at("s_ones", [1, 128], F32, offset=26624)
    allocs.append(("s_ones", 26624, 512, 0, 4))
    s_bqkv = sb("s_bqkv", 8, F32, 27136, 0, 4)
    s_bproj = sb("s_bproj", 8, F32, 27168, 0, 4)
    s_stats = sb("s_stats", 48, F32, 27200, 0, 4)

    s_QT = sb("s_QT", 16384, BF16, 27392, 0, 1)
    s_KT = sb("s_KT", 16384, BF16, 60160, 0, 1)
    s_VT = sb("s_VT", 16640, BF16, 92928, 0, 1)
    s_yroll = sb("s_yroll", 4096, BF16, 126208, 0, 2)
    s_xbuf = sb("s_xbuf", 4096, F32, 134400, 0, 1)
    s_xnst = sb("s_xnst", 4096, BF16, 150784, 0, 1)
    s_xntb = sb("s_xntb", 4096, BF16, 158976, 0, 1)
    s_wqkvb = sb("s_wqkvb", 6144, BF16, 167168, 0, 1)
    s_vstage = sb("s_vstage", 2048, BF16, 179456, 0, 1)
    s_ptb = sb("s_ptb", 4096, BF16, 183552, 0, 2)
    s_deninv = sb("s_deninv", 1024, F32, 191744, 0, 2)
    s_ycp = sb("s_ycp", 1024, F32, 195840, 0, 2)
    s_spstg = sb("s_spstg", 2048, F32, 199936, 0, 2)
    s_wprojb = sb("s_wprojb", 2048, BF16, 208128, 0, 2)
    s_ostg = sb("s_ostg", 4096, BF16, 212224, 0, 1)
    s_dbc = sb("s_dbc", 1024, F32, 220416, 0, 2)

    s_xnt_own = sb("s_xnt_own", 16384, BF16, 27392, 3, 4)
    s_wdb = sb("s_wdb", 24576, BF16, 64256, 3, 3)
    s_wgu = sb("s_wgu", 6144, BF16, 113408, 3, 3)
    s_sg = sb("s_sg", 2048, F32, 125696, 3, 3)
    s_hhh = sb("s_hhh", 4096, BF16, 133888, 3, 3)
    s_ht = sb("s_ht", 24576, BF16, 142080, 3, 3)
    s_gstg = sb("s_gstg", 1024, F32, 191232, 3, 3)

    s_s2 = sb("s_s2", 1024, F32, 195328, 3, 4)
    s_pt = sb("s_pt", 1024, BF16, 199424, 3, 4)
    s_ptT = sb("s_ptT", 2048, BF16, 201472, 3, 4)
    s_xo = sb("s_xo", 2048, F32, 205568, 3, 4)
    s_outt = sb("s_outt", 2048, F32, 213760, 3, 4)

    with ExitStack() as ctx:
        main = ctx.enter_context(nc.psum_tensor("main", [128, 3072], F32))
        ptA = ctx.enter_context(nc.psum_tensor("ptA", [128, 1024], BF16))
        ptB = ctx.enter_context(nc.psum_tensor("ptB", [128, 1024], BF16))
        s = Sched(nc, ctx)
        block = ctx.enter_context(nc.Block())

        s.dma("sync", s_ident[:], ident[:, :], sem="dcst")
        s.dma("sync", s_maskw[:], maskw[:, :], sem="dcst")
        s.dma("sync", s_ones[:], ones1[:, :], sem="dcst")
        s.dma("sync", s_bqkv[:, 0:6], bqkv[:, :], sem="dcst")
        s.dma("sync", s_bproj[:, 0:8], bproj[:, :], sem="dcst")
        ds_consts = ("dcst", s.cnt["dcst"])
        s.dma("gpsimd", wqkv_bf.ap()[:, :], wqkv.ap()[:, :], sem="dcast")
        s.dma("gpsimd", wproj_bf.ap()[:, :], wproj.ap()[:, :], sem="dcast")
        dp_cast0 = s.cnt["dcast"]
        if SHUF_EN:
          s.dma("gpsimd", wqkv_blk.ap().rearrange("p (a d) -> p a d", d=768),
              wqkv_bf.ap().rearrange("(a p) d -> p a d", p=128),
              wait=[("dcast", dp_cast0)], sem="dcast")
          s.dma("gpsimd", wproj_blk.ap().rearrange("p (a d) -> p a d", d=1024),
              wproj_bf.ap().rearrange("(a p) d -> p a d", p=128),
              wait=[("dcast", dp_cast0)], sem="dcast")
        dp_wcast = s.cnt["dcast"]
        s.dma("scalar", s_wqkvb[:], wqkv_blk.ap()[:, :],
              wait=[("dcast", dp_wcast)], sem="dwq")
        da_wqkv = ("dwq", s.cnt["dwq"])
        s.dma("scalar", s_wprojb[:], wproj_blk.ap()[:, :],
              wait=[("dcast", dp_wcast)], sem="dwp")
        da_wproj = ("dwp", s.cnt["dwp"])
        for half in range(2):
            s.dma("gpsimd", wg_bf.ap()[half * 512:(half + 1) * 512, :],
                  wgT.ap()[half * 512:(half + 1) * 512, :], sem="dcastp")
            s.dma("gpsimd", wu_bf.ap()[half * 512:(half + 1) * 512, :],
                  wuT.ap()[half * 512:(half + 1) * 512, :], sem="dcastp")
            s.dma("gpsimd", wd_bf.ap()[half * 1536:(half + 1) * 1536, :],
                  wdT.ap()[half * 1536:(half + 1) * 1536, :], sem="dcastp")
        dp_castp = s.cnt.get("dcastp", 0)
        for hh in range(24 if SHUF_EN else 0):
            r0, r1 = hh * 128, (hh + 1) * 128
            s.dma("gpsimd",
                  wgu_blk.ap()[r0:r1, 0:1024].rearrange("p (a h) -> p a h", h=128),
                  wg_bf.ap()[:, hh * 128:(hh + 1) * 128]
                  .rearrange("(a p) h -> p a h", p=128),
                  wait=[("dcastp", dp_castp)] if hh == 0 else [],
                  sem="dcastf")
            s.dma("gpsimd",
                  wgu_blk.ap()[r0:r1, 1024:2048].rearrange("p (a h) -> p a h", h=128),
                  wu_bf.ap()[:, hh * 128:(hh + 1) * 128]
                  .rearrange("(a p) h -> p a h", p=128),
                  sem="dcastf")
        for part in range(4 if SHUF_EN else 0):
            s.dma("gpsimd",
                  wd_blk.ap().rearrange("p (a c) -> p a c", c=1024)
                  [:, part * 6:(part + 1) * 6, :],
                  wd_bf.ap().rearrange("(a p) c -> p a c", p=128)
                  [:, part * 6:(part + 1) * 6, :],
                  sem="dcastf")
        dp_wbf = s.cnt.get("dcastf", 0)
        s.add("vector", lambda q: q.memset(s_VT[:], 1.0), inc=[("dv", 1)])
        dv_vtones = s.cnt["dv"]

        xcons_dv = [0, 0, 0, 0]
        xnst_pe = [0, 0, 0, 0]
        xnst_w_dv = [0, 0, 0, 0]
        xnst_sq_ac = [0, 0, 0, 0]
        x_loaded_ds = [0, 0, 0, 0]
        stats_free_dv = [0, 0, 0, 0]

        def ln_stage1(ti, src_rows):
            slot = ti % 4
            sb_ = slot * 10
            st = s_stats
            xsem = "dx%d" % slot
            x_loaded_ds[slot] = s.dma(
                "sync", s_xbuf[:, slot * 1024:(slot + 1) * 1024], src_rows,
                wait=[("dv", xcons_dv[slot]), ("ac", xnst_sq_ac[slot])], sem=xsem)
            xs = s_xbuf[:, slot * 1024:(slot + 1) * 1024]
            s.add("vector", lambda q, xs=xs, o=st[:, sb_:sb_ + 1]: q.reduce_sum(o, xs, axis=AX.X),
                  inc=[("dv", 1)],
                  wait=[(xsem, x_loaded_ds[slot]),
                        ("dv", max(stats_free_dv[slot], s.cnt["dv"]))])
            s.add("scalar", lambda q, xs=xs, o=st[:, sb_ + 1:sb_ + 2],
                  d_=s_xnst[:, slot * 1024:(slot + 1) * 1024]: q.activation(
                d_, xs, AF.Square, accum_out=o),
                inc=[("ac", 1)],
                wait=[(xsem, x_loaded_ds[slot]), ("dv", stats_free_dv[slot]),
                      ("ac", s.cnt["ac"]), ("pe", xnst_pe[slot]),
                      ("dv", xnst_w_dv[slot])])
            xnst_sq_ac[slot] = s.cnt["ac"]
            return s.cnt["dv"], s.cnt["ac"]

        def ln_stage2_group(tis, marks):
            def slots(ti):
                slot = ti % 4
                sb_ = slot * 10
                st = s_stats
                return slot, {
                    "s12": st[:, sb_ + 0:sb_ + 2], "mex": st[:, sb_ + 2:sb_ + 4],
                    "m": st[:, sb_ + 2:sb_ + 3], "ex2": st[:, sb_ + 3:sb_ + 4],
                    "nvar": st[:, sb_ + 5:sb_ + 6], "sd": st[:, sb_ + 6:sb_ + 7],
                    "rstd": st[:, sb_ + 7:sb_ + 8], "negb": st[:, sb_ + 8:sb_ + 9],
                    "echo": st[:, sb_ + 9:sb_ + 10],
                    "rn": st[:, sb_ + 7:sb_ + 9]}
            S_ = [slots(ti) for ti in tis]
            for (sl, a), (dv_red, ac_sq) in zip(S_, marks):
                s.add("scalar", lambda q, o=a["echo"], i_=s_stats[:, sl * 10 + 1:sl * 10 + 2]:
                      q.activation(o, i_, AF.Identity),
                      inc=[("ac", 1)], wait=[("ac", ac_sq)])
            ac_e = s.cnt["ac"]
            mex_marks = []
            for (sl, a), (dv_red, ac_sq) in zip(S_, marks):
                s.add("vector", lambda q, o=a["mex"], i_=a["s12"]: q.tensor_scalar_mul(
                    o, i_, 1.0 / 1024.0),
                    inc=[("dv", 1)], wait=[("ac", ac_e), ("dv", dv_red)])
                mex_marks.append(s.cnt["dv"])
            for (sl, a), mk in zip(S_, mex_marks):
                s.add("vector", lambda q, o=a["nvar"], i_=a["m"], e_=a["ex2"]:
                      q.scalar_tensor_tensor(o, i_, i_, e_, ALU.mult, ALU.subtract),
                      inc=[("dv", 1)], wait=[("dv", mk)])
            dv_nv = s.cnt["dv"]
            for (sl, a) in S_:
                s.add("vector", lambda q, o=a["echo"], i_=a["nvar"]: q.tensor_copy(o, i_),
                      inc=[("dv", 1)], wait=[("dv", dv_nv)])
            dv_nve = s.cnt["dv"]
            for (sl, a) in S_:
                s.add("scalar", lambda q, o=a["sd"], i_=a["nvar"]: q.activation(
                    o, i_, AF.Sqrt, scale=-1.0, bias=np.float32(1e-5)),
                    inc=[("ac", 1)], wait=[("dv", dv_nve), ("ac", s.cnt["ac"])])
            ac_sd = s.cnt["ac"]
            for (sl, a) in S_:
                s.add("scalar", lambda q, o=a["echo"], i_=a["sd"]: q.activation(
                    o, i_, AF.Identity),
                    inc=[("ac", 1)], wait=[("ac", ac_sd)])
            ac_sde = s.cnt["ac"]
            for (sl, a) in S_:
                s.add("vector", lambda q, o=a["rstd"], i_=a["sd"]: q.reciprocal(o, i_),
                      inc=[("dv", 1)], wait=[("ac", ac_sde), ("dv", s.cnt["dv"])])
            dv_rs = s.cnt["dv"]
            for (sl, a) in S_:
                s.add("vector", lambda q, o=a["negb"], i_=a["m"], r_=a["rstd"]:
                      q.scalar_tensor_tensor(o, i_, -1.0, r_, ALU.mult, ALU.mult),
                      inc=[("dv", 1)], wait=[("dv", dv_rs)])
            dv_nb = s.cnt["dv"]
            for (sl, a) in S_:
                s.add("vector", lambda q, o=a["echo"], i_=a["rn"]: q.reduce_sum(
                    o, i_, axis=AX.X),
                    inc=[("dv", 1)], wait=[("dv", max(dv_nb, s.cnt["dv"]))])
            dv_nbe = s.cnt["dv"]
            last = 0
            for (sl, a), ti in zip(S_, tis):
                xs = s_xbuf[:, sl * 1024:(sl + 1) * 1024]
                xo = s_xnst[:, sl * 1024:(sl + 1) * 1024]
                s.add("vector", lambda q, xs=xs, xo=xo, sc_=a["rstd"], b_=a["negb"]:
                      q.tensor_scalar(xo, xs, sc_, b_, ALU.mult, ALU.add),
                      inc=[("dv", 1)],
                      wait=[("dv", dv_nbe), ("pe", xnst_pe[sl]),
                            ("ac", xnst_sq_ac[sl])])
                xcons_dv[sl] = s.cnt["dv"]
                stats_free_dv[sl] = s.cnt["dv"]
                xnst_w_dv[sl] = s.cnt["dv"]
                last = s.cnt["dv"]
            return last

        def ln_group(tis, srcs):
            marks = [ln_stage1(ti, sr) for ti, sr in zip(tis, srcs)]
            return ln_stage2_group(tis, marks)

        tslot_dv = [dv_vtones, dv_vtones]
        tslot_idx = [0]

        def transpose_batch(srcs, evac_fn, waits, evac_tile=None):
            banks = (ptA, ptB)
            if TB_BATCH:
                sl = tslot_idx[0] % 2
                tslot_idx[0] += 1
                bank = banks[sl]
                for i, src in enumerate(srcs):
                    w = (list(waits) + [("dv", tslot_dv[sl]), ds_consts]) if i == 0 \
                        else []
                    s.add("tensor", lambda q, o=bank[:, i * 128:(i + 1) * 128],
                          i_=src: q.transpose(o, i_, s_ident[:]),
                          inc=[("pe", 1)], wait=w)
                pe_tr = s.cnt["pe"]
                evac_fn(bank[:, 0:512], pe_tr)
                tslot_dv[sl] = s.cnt["dv"]
            else:
                for i, src in enumerate(srcs):
                    sl = tslot_idx[0] % 2
                    tslot_idx[0] += 1
                    bank = banks[sl]
                    w = (list(waits) if i == 0 else []) + \
                        [("dv", tslot_dv[sl]), ds_consts]
                    s.add("tensor", lambda q, o=bank[:, 0:128],
                          i_=src: q.transpose(o, i_, s_ident[:]),
                          inc=[("pe", 1)], wait=w)
                    evac_tile(i, bank[:, 0:128], s.cnt["pe"])
                    tslot_dv[sl] = s.cnt["dv"]

        xntb_pe = [0]
        qkvps_dv = [0]
        vstage_dv = [0]
        s.mksem("dxt")

        def qkv_half(g, ac_ready):
            for ch in range(8):
                srcs = [s_xnst[:, i * 1024 + ch * 128: i * 1024 + (ch + 1) * 128]
                        for i in range(4)]

                def evac(slot_ap, pe_tr, ch=ch):
                    w = [("pe", pe_tr)]
                    if ch < 2:
                        w.append(("pe", xntb_pe[0]))
                    s.add("vector", lambda q, d=s_xntb[:, ch * 512:(ch + 1) * 512],
                          s2=slot_ap: q.tensor_copy(d, s2),
                          inc=[("dv", 1)], wait=w)

                def evac_t(i, tile_ap, pe_tr, ch=ch):
                    w = [("pe", pe_tr)]
                    if ch == 0 and i == 0:
                        w.append(("pe", xntb_pe[0]))
                    s.add("vector", lambda q,
                          d=s_xntb[:, ch * 512 + i * 128: ch * 512 + (i + 1) * 128],
                          s2=tile_ap: q.tensor_copy(d, s2),
                          inc=[("dv", 1)], wait=w)
                transpose_batch(srcs, evac, [("dv", ac_ready)], evac_t)
            da = s.cnt["dv"]
            for sl in range(4):
                xnst_pe[sl] = s.cnt["pe"]
            t0 = g * 512
            for dp_ in range(6):
                ps = main[:, 0:512]
                for ch in range(8):
                    w = [("dv", da), da_wqkv,
                         ("dv", max(qkvps_dv[0], b2_free_dv[0], last_bc[0])),
                         ("ac", last_exp_ac[0])] if ch == 0 else []
                    st_ap = s_wqkvb[:, ch * 768 + dp_ * 128: ch * 768 + (dp_ + 1) * 128]
                    rhs = s_xntb[:, ch * 512:(ch + 1) * 512]
                    s.add("tensor", lambda q, ps=ps, st_ap=st_ap, rhs=rhs, ch=ch:
                          q.matmul(ps, st_ap, rhs, start=(ch == 0), stop=(ch == 7)),
                          inc=[("pe", 1)], wait=w)
                pe_mm = s.cnt["pe"]
                xntb_pe[0] = pe_mm
                if dp_ < 2:
                    dst = s_QT[:, dp_ * 8192 + t0: dp_ * 8192 + t0 + 512]
                elif dp_ < 4:
                    dst = s_KT[:, (dp_ - 2) * 8192 + t0: (dp_ - 2) * 8192 + t0 + 512]
                else:
                    dst = s_vstage[:, (dp_ - 4) * 512:(dp_ - 4) * 512 + 512]
                bias_ap = s_bqkv[:, dp_:dp_ + 1]
                vw = [("pe", pe_mm)]
                if dp_ >= 4:
                    vw.append(("dv", vstage_dv[0]))
                s.add("vector", lambda q, dst=dst, ps=ps, b=bias_ap:
                      q.tensor_scalar(dst, ps, b, None, ALU.add),
                      inc=[("dv", 1)], wait=vw)
                qkvps_dv[0] = s.cnt["dv"]
                if dp_ >= 4:
                    pair = dp_ - 4
                    dv_evac = s.cnt["dv"]
                    srcs = [s_vstage[:, pair * 512 + i * 128: pair * 512 + (i + 1) * 128]
                            for i in range(4)]

                    def evacV(slot_ap, pe_tr, g=g, pair=pair):
                        for i in range(4):
                            kt = g * 4 + i
                            base = kt * 260 + 130 * pair
                            dst2 = s_VT[:, base:base + 130].rearrange(
                                "p (h x) -> p h x", x=65)[:, 0:2, 0:64]
                            src2 = slot_ap[:, i * 128:(i + 1) * 128].rearrange(
                                "p (h x) -> p h x", x=64)
                            s.add("vector", lambda q, d=dst2, s2=src2:
                                  q.tensor_copy(d, s2),
                                  inc=[("dv", 1)],
                                  wait=[("pe", pe_tr)] if i == 0 else [])

                    def evacV_t(i, tile_ap, pe_tr, g=g, pair=pair):
                        kt = g * 4 + i
                        base = kt * 260 + 130 * pair
                        dst2 = s_VT[:, base:base + 130].rearrange(
                            "p (h x) -> p h x", x=65)[:, 0:2, 0:64]
                        src2 = tile_ap.rearrange("p (h x) -> p h x", x=64)
                        s.add("vector", lambda q, d=dst2, s2=src2:
                              q.tensor_copy(d, s2),
                              inc=[("dv", 1)], wait=[("pe", pe_tr)])
                    transpose_batch(srcs, evacV, [("dv", dv_evac)], evacV_t)
                    vstage_dv[0] = s.cnt["dv"]

        ostg_dp = [0]
        dxt_cnt = [0]

        def own_chunk(oi):
            ac_own = ln_group([64 + oi * 4 + t for t in range(4)],
                              [x_own.ap()[(oi * 4 + t) * 128:(oi * 4 + t + 1) * 128, :]
                               for t in range(4)])
            for ch in range(8):
                srcs = [s_xnst[:, i * 1024 + ch * 128: i * 1024 + (ch + 1) * 128]
                        for i in range(4)]

                def evacO(slot_ap, pe_tr, ch=ch):
                    s.add("vector", lambda q, d=s_ostg[:, ch * 512:(ch + 1) * 512],
                          s2=slot_ap: q.tensor_copy(d, s2),
                          inc=[("dv", 1)],
                          wait=[("pe", pe_tr), ("dxt", ostg_dp[0])])

                def evacO_t(i, tile_ap, pe_tr, ch=ch):
                    s.add("vector", lambda q,
                          d=s_ostg[:, ch * 512 + i * 128: ch * 512 + (i + 1) * 128],
                          s2=tile_ap: q.tensor_copy(d, s2),
                          inc=[("dv", 1)],
                          wait=[("pe", pe_tr), ("dxt", ostg_dp[0])])
                transpose_batch(srcs, evacO, [("dv", ac_own)], evacO_t)
            for sl in range(4):
                xnst_pe[sl] = s.cnt["pe"]
            ostg_dp[0] = s.dma(
                "gpsimd",
                xnt_own_dram.ap().rearrange("p (ch t) -> p ch t", t=2048)
                [:, :, oi * 512:(oi + 1) * 512],
                s_ostg[:].rearrange("p (ch t) -> p ch t", t=512),
                wait=[("dv", s.cnt["dv"])], sem="dxt")
            dxt_cnt[0] = s.cnt["dxt"]

        grp_done_dv = {}
        last_exp_ac = [0, 0]
        last_av_pe = [0, 0, 0, 0]
        last_av_cons_dv = 0
        last_ymul_dv = 0
        b2_free_dv = [0]
        last_bc = [0]
        b2_yroll_pe = [0, 0, 0, 0]

        stb = [main[:, 0:1024], main[:, 1024:2048]]
        avps = main[:, 2048:3072]

        def emit_qk(qc, pair, kt):
            b_ = kt % 2
            w = [("ac", last_exp_ac[b_]), ("dv", grp_done_dv[qc])]
            if kt <= 1:
                w.append(("dv", max(b2_free_dv[0], last_bc[0])))
            stA = s_KT[0:64, pair * 8192 + kt * 128: pair * 8192 + (kt + 1) * 128]
            stB = s_KT[64:128, pair * 8192 + kt * 128: pair * 8192 + (kt + 1) * 128]
            rhA = s_QT[0:64, pair * 8192 + qc * 512: pair * 8192 + (qc + 1) * 512]
            rhB = s_QT[64:128, pair * 8192 + qc * 512: pair * 8192 + (qc + 1) * 512]
            s.add("tensor", lambda q, o=stb[b_][:, 0:512], a=stA, r=rhA:
                  q.matmul(o, a, r, start=True, stop=True),
                  inc=[("pe", 1)], wait=w)
            s.add("tensor", lambda q, o=stb[b_][:, 512:1024], a=stB, r=rhB:
                  q.matmul(o, a, r, start=True, stop=True, tile_position=(64, 0)),
                  inc=[("pe", 1)])
            return s.cnt["pe"]

        exp_info = {}

        def emit_exp(qc, pair, kt, pe_qk):
            b_ = kt % 2
            p4 = kt % 4
            pt = s_ptb[:, p4 * 1024:(p4 + 1) * 1024]
            s.add("scalar", lambda q, o=pt, i_=stb[b_][:, 0:1024]:
                  q.activation(o, i_, AF.Exp, scale=SCALE),
                  inc=[("ac", 1)],
                  wait=[("pe", max(pe_qk, last_av_pe[p4]))])
            last_exp_ac[b_] = s.cnt["ac"]
            ac_exp = s.cnt["ac"]
            dv_msk = 0
            if kt >= 4 * qc:
                o = kt * 128 - qc * 512
                msk = s_maskw[:, 384 - o: 896 - o]
                s.add("vector", lambda q, p0=pt[:, 0:512], m=msk:
                      q.tensor_mul(p0, p0, m), inc=[("dv", 1)],
                      wait=[("ac", ac_exp)])
                s.add("vector", lambda q, p1=pt[:, 512:1024], m=msk:
                      q.tensor_mul(p1, p1, m), inc=[("dv", 1)])
                dv_msk = s.cnt["dv"]
            exp_info[kt] = (ac_exp, dv_msk)

        def emit_av(qc, pair, kt, K):
            p4 = kt % 4
            pt = s_ptb[:, p4 * 1024:(p4 + 1) * 1024]
            ac_exp, dv_msk = exp_info[kt]
            avw = [("ac", ac_exp)]
            if dv_msk:
                avw.append(("dv", dv_msk))
            if kt == 0:
                avw.append(("dv", max(last_av_cons_dv, last_ymul_dv)))
            vA = s_VT[:, kt * 260 + 130 * pair: kt * 260 + 130 * pair + 65]
            vB = s_VT[:, kt * 260 + 130 * pair + 65: kt * 260 + 130 * pair + 130]
            s.add("tensor", lambda q, v=vA, p=pt[:, 0:512], st_=(kt == 0), sp=(kt == K - 1):
                  q.matmul(avps[0:65, 0:512], v, p, start=st_, stop=sp),
                  inc=[("pe", 1)], wait=avw)
            s.add("tensor", lambda q, v=vB, p=pt[:, 512:1024], st_=(kt == 0), sp=(kt == K - 1):
                  q.matmul(avps[0:65, 512:1024], v, p, start=st_, stop=sp),
                  inc=[("pe", 1)])
            last_av_pe[p4] = s.cnt["pe"]

        def attn_qc(qc):
            nonlocal last_av_cons_dv, last_ymul_dv
            for pair in range(2):
                K = 4 * qc + 4
                pe_qks = {}
                for kt in range(K):
                    pe_qks[kt] = emit_qk(qc, pair, kt)
                    emit_exp(qc, pair, kt, pe_qks[kt])
                    if kt >= 2:
                        emit_av(qc, pair, kt - 2, K)
                emit_av(qc, pair, K - 2, K)
                emit_av(qc, pair, K - 1, K)
                pe_av_done = s.cnt["pe"]
                s.add("vector", lambda q: q.tensor_copy(
                    s_ycp[0:64, 0:512], avps[0:64, 0:512]),
                    inc=[("dv", 1)], wait=[("pe", pe_av_done)])
                s.add("vector", lambda q: q.tensor_copy(
                    s_ycp[64:128, 0:512], avps[0:64, 512:1024]), inc=[("dv", 1)])
                s.add("vector", lambda q: q.tensor_copy(
                    s_deninv[0:1, 0:512], avps[64:65, 0:512]), inc=[("dv", 1)])
                s.add("vector", lambda q: q.tensor_copy(
                    s_deninv[0:1, 512:1024], avps[64:65, 512:1024]),
                    inc=[("dv", 1)])
                dv_den = s.cnt["dv"]
                last_av_cons_dv = dv_den
                s.add("tensor", lambda q: q.matmul(
                    avps[:, 0:512], s_ones[0:1, :], s_deninv[0:1, 0:512],
                    start=True, stop=True),
                    inc=[("pe", 1)], wait=[("dv", dv_den)])
                s.add("tensor", lambda q: q.matmul(
                    avps[:, 512:1024], s_ones[0:1, :], s_deninv[0:1, 512:1024],
                    start=True, stop=True), inc=[("pe", 1)])
                pe_bc = s.cnt["pe"]
                s.add("vector", lambda q: q.reciprocal(
                    s_dbc[:, 0:512], avps[:, 0:512]),
                    inc=[("dv", 1)], wait=[("pe", pe_bc), ("dv", s.cnt["dv"])])
                s.add("vector", lambda q: q.reciprocal(
                    s_dbc[:, 512:1024], avps[:, 512:1024]),
                    inc=[("dv", 1)], wait=[("dv", s.cnt["dv"])])
                dv_rec = s.cnt["dv"]
                last_av_cons_dv = dv_rec
                sl4 = qc % 4
                yb = sl4 * 1024 + pair * 512
                yslA = s_yroll[0:64, yb:yb + 512]
                yslB = s_yroll[64:128, yb:yb + 512]
                s.add("vector", lambda q, o=yslA, i0=s_ycp[0:64, 0:512],
                      i1=s_dbc[0:64, 0:512]: q.tensor_mul(o, i0, i1),
                      inc=[("dv", 1)],
                      wait=[("dv", dv_rec), ("pe", b2_yroll_pe[sl4])])
                s.add("vector", lambda q, o=yslB, i0=s_ycp[64:128, 0:512],
                      i1=s_dbc[64:128, 512:1024]: q.tensor_mul(o, i0, i1),
                      inc=[("dv", 1)])
                last_ymul_dv = s.cnt["dv"]

        b2_dv = [0, 0, 0, 0]
        b2_ds = [0, 0, 0, 0]
        for _k in range(4):
            s.mksem("drs%d" % _k)

        def b2_chunk(g):
            sl4 = g % 4
            for cp in range(8):
                k = cp % 4
                ps = main[:, k * 512:(k + 1) * 512]
                for pr in range(2):
                    w = []
                    if pr == 0:
                        w = [("dv", max(b2_dv[k], last_ymul_dv)), da_wproj,
                             ("ac", max(last_exp_ac[0], last_exp_ac[1]))]
                    st_ap = s_wprojb[:, pr * 1024 + cp * 128: pr * 1024 + (cp + 1) * 128]
                    rhs = s_yroll[:, sl4 * 1024 + pr * 512: sl4 * 1024 + (pr + 1) * 512]
                    s.add("tensor", lambda q, ps=ps, a=st_ap, r=rhs, pr=pr:
                          q.matmul(ps, a, r, start=(pr == 0), stop=(pr == 1)),
                          inc=[("pe", 1)], wait=w)
                pe_mm = s.cnt["pe"]
                stg = s_spstg[:, k * 512:(k + 1) * 512]
                s.add("vector", lambda q, o=stg, i_=ps: q.tensor_copy(o, i_),
                      inc=[("dv", 1)], wait=[("pe", pe_mm), ("drs%d" % k, b2_ds[k])])
                b2_dv[k] = s.cnt["dv"]
                rb = 1024 * (g // 4) + cp * 128
                cb = (g % 4) * 512
                b2_ds[k] = s.dma("sync", rs_in.ap()[rb:rb + 128, cb:cb + 512], stg,
                                 wait=[("dv", b2_dv[k])], sem="drs%d" % k)
            b2_yroll_pe[sl4] = s.cnt["pe"]
            b2_free_dv[0] = max(b2_dv)

        OWN_AT = {8: 0, 10: 1, 12: 2, 14: 3}
        ac_ready = ln_group([t for t in range(4)],
                            [x_full.ap()[ti * 128:(ti + 1) * 128, :]
                             for ti in range(4)])
        for g in range(NGRP):
            qkv_half(g, ac_ready)
            grp_done_dv[g] = s.cnt["dv"]
            if OWN_EN and g in OWN_AT:
                own_chunk(OWN_AT[g])
            if g < 15:
                tis = [(g + 1) * 4 + t for t in range(4)]
                ac_ready = ln_group(
                    tis, [x_full.ap()[ti * 128:(ti + 1) * 128, :] for ti in tis])
            attn_qc(g)
            if B2_EN:
                b2_chunk(g)
        da_xnt_own = dxt_cnt[0]
        ds_rs_in = [("drs%d" % _k, s.cnt["drs%d" % _k]) for _k in range(4)]
        pe_B_end = s.cnt["pe"]
        dv_b2_end = s.cnt["dv"]
        if not TAIL_EN:
            final_waits = [(n, s.cnt[n]) for n in list(s.cnt.keys())
                           if n not in ("pe", "dv", "ac", "cc") and s.cnt[n] > 0]
            s.add("sync", lambda q: q.nop(), wait=final_waits)
            s.add("sync", lambda q: q.dma_start(
                out=out_own.ap()[0:128, :], in_=s_xbuf[:, 0:1024]),
                inc=[("ds", 16)])
            s.add("sync", lambda q: q.nop(), wait=[("ds", s.cnt["ds"])])
            s.emit(block)
            return nc

        wgu_da = []
        for hh in range(2):
            wsem = "dwg%d" % (hh % 3)
            s.dma("scalar", s_wgu[:, (hh % 3) * 2048:(hh % 3) * 2048 + 1024],
                  wgu_blk.ap()[hh * 128:(hh + 1) * 128, 0:1024],
                  wait=[("pe", pe_B_end), ("dcastf", dp_wbf)], sem=wsem)
            s.dma("gpsimd", s_wgu[:, (hh % 3) * 2048 + 1024:(hh % 3) * 2048 + 2048],
                  wgu_blk.ap()[hh * 128:(hh + 1) * 128, 1024:2048],
                  wait=[("pe", pe_B_end), ("dcastf", dp_wbf)], sem=wsem)
            wgu_da.append((wsem, s.cnt[wsem]))
        s.add("gpsimd", lambda q: q.collective_compute(
            "ReduceScatter", ALU.add,
            replica_groups=[[0, 1, 2, 3], [4, 5, 6, 7]],
            ins=[rs_in.ap().opt()], outs=[rs_out.ap().opt()]),
            inc=[("cc", 1)], wait=ds_rs_in)
        for part in range(4):
            s.dma("sync", s_wdb[:, part * 6144:(part + 1) * 6144],
                  wd_blk.ap()[:, part * 6144:(part + 1) * 6144],
                  wait=[("pe", pe_B_end), ("dcastf", dp_wbf)], sem="dwd")
        da_wd = ("dwd", s.cnt["dwd"])

        s.dma("gpsimd", s_xnt_own[:], xnt_own_dram.ap()[:, :],
              wait=[("dxt", da_xnt_own), ("pe", pe_B_end)], sem="dxc")
        da_xnt_own_c = s.cnt["dxc"]
        gate_bank_ac = [0, 0]
        up_bank_dv = [0, 0]
        sg_dv = [0, 0, 0, 0]
        hhh_ds = [0, 0]
        gu_pe = [0, 0, 0]
        for hh in range(24):
            if hh >= 2:
                wsem = "dwg%d" % (hh % 3)
                s.dma("scalar", s_wgu[:, (hh % 3) * 2048:(hh % 3) * 2048 + 768],
                      wgu_blk.ap()[hh * 128:(hh + 1) * 128, 0:768],
                      wait=[("pe", gu_pe[hh % 3])], sem=wsem)
                s.dma("gpsimd", s_wgu[:, (hh % 3) * 2048 + 768:(hh % 3) * 2048 + 1536],
                      wgu_blk.ap()[hh * 128:(hh + 1) * 128, 768:1536],
                      wait=[("pe", gu_pe[hh % 3])], sem=wsem)
                s.dma("sync", s_wgu[:, (hh % 3) * 2048 + 1536:(hh % 3) * 2048 + 2048],
                      wgu_blk.ap()[hh * 128:(hh + 1) * 128, 1536:2048],
                      wait=[("pe", gu_pe[hh % 3])], sem=wsem)
                wgu_da.append((wsem, s.cnt[wsem]))
            hb = hh % 2
            dv_last_stt = 0
            for tc in range(4):
                tb = tc % 2
                pg = main[:, tb * 512:(tb + 1) * 512]
                pu = main[:, 1024 + tb * 512: 1024 + (tb + 1) * 512]
                for ch in range(8):
                    w = [wgu_da[hh], ("dxc", da_xnt_own_c), ("ac", gate_bank_ac[tb]),
                         ("dv", dv_b2_end)] if ch == 0 else []
                    st_ap = s_wgu[:, (hh % 3) * 2048 + ch * 128:(hh % 3) * 2048 + (ch + 1) * 128]
                    rhs = s_xnt_own[:, ch * 2048 + tc * 512: ch * 2048 + (tc + 1) * 512]
                    s.add("tensor", lambda q, ps=pg, a=st_ap, r=rhs, ch=ch:
                          q.matmul(ps, a, r, start=(ch == 0), stop=(ch == 7)),
                          inc=[("pe", 1)], wait=w)
                pe_g = s.cnt["pe"]
                for ch in range(8):
                    w = [("dv", up_bank_dv[tb])] if ch == 0 else []
                    st_ap = s_wgu[:, (hh % 3) * 2048 + 1024 + ch * 128:
                                  (hh % 3) * 2048 + 1024 + (ch + 1) * 128]
                    rhs = s_xnt_own[:, ch * 2048 + tc * 512: ch * 2048 + (tc + 1) * 512]
                    s.add("tensor", lambda q, ps=pu, a=st_ap, r=rhs, ch=ch:
                          q.matmul(ps, a, r, start=(ch == 0), stop=(ch == 7)),
                          inc=[("pe", 1)], wait=w)
                pe_u = s.cnt["pe"]
                gu_pe[hh % 3] = pe_u
                sgs = s_sg[:, tc * 512:(tc + 1) * 512]
                s.add("scalar", lambda q, o=sgs, i_=pg: q.activation(
                    o, i_, AF.Silu, scale=S127),
                    inc=[("ac", 1)], wait=[("pe", pe_g), ("dv", sg_dv[tc])])
                gate_bank_ac[tb] = s.cnt["ac"]
                ac_sg = s.cnt["ac"]
                hsl = s_hhh[:, hb * 2048 + tc * 512: hb * 2048 + (tc + 1) * 512]
                s.add("vector", lambda q, o=hsl, i_=pu, g_=sgs: q.scalar_tensor_tensor(
                    o, i_, S127, g_, ALU.mult, ALU.mult),
                    inc=[("dv", 1)],
                    wait=[("ac", ac_sg), ("pe", pe_u), ("dh%d" % hb, hhh_ds[hb])])
                up_bank_dv[tb] = s.cnt["dv"]
                sg_dv[tc] = s.cnt["dv"]
                dv_last_stt = s.cnt["dv"]
            hhh_ds[hb] = s.dma("sync", h_dram.ap()[hh * 128:(hh + 1) * 128, :],
                               s_hhh[:, hb * 2048:(hb + 1) * 2048],
                               wait=[("dv", dv_last_stt)], sem="dh%d" % hb)

        hh_store = [s.cnt.get("dh0", 0), s.cnt.get("dh1", 0)]
        ht_ds = [0, 0]
        ht_pe = [0, 0]
        dps_ac = [0, 0]
        gstg_dv = [0, 0]
        s2_ds = [0, 0]
        s2_dv = [0, 0]
        p_dv = [("dpp0", 0), ("dpp1", 0)]
        out_dv = [0, 0]
        out_st = [0, 0]
        dpt_mark = [0, 0]
        for tc in range(4):
            tb = tc % 2
            psem = "dpp%d" % tb
            s.dma("scalar",
                  s_ht[:, tb * 12288:tb * 12288 + 6144]
                  .rearrange("p (a t) -> p a t", t=512),
                  h_dram.ap()[0:1536, tc * 512:(tc + 1) * 512]
                  .rearrange("(a p) t -> p a t", p=128),
                  wait=[("dh0", hh_store[0]), ("dh1", hh_store[1]),
                        ("pe", ht_pe[tb])],
                  sem="dht%d" % tb)
            ht_ds[tb] = s.dma("gpsimd",
                              s_ht[:, tb * 12288 + 6144:(tb + 1) * 12288]
                              .rearrange("p (a t) -> p a t", t=512),
                              h_dram.ap()[1536:3072, tc * 512:(tc + 1) * 512]
                              .rearrange("(a p) t -> p a t", p=128),
                              wait=[("dh0", hh_store[0]), ("dh1", hh_store[1]),
                                    ("pe", ht_pe[tb])],
                              sem="dht%d" % tb)
            for cp in range(8):
                k = (tc * 8 + cp) % 2
                s2_ds[k] = s.dma("gpsimd", s_s2[:, k * 512:(k + 1) * 512],
                                 rs_out.ap()[cp * 128:(cp + 1) * 128,
                                             tc * 512:(tc + 1) * 512],
                                 wait=[("cc", 1), ("dv", s2_dv[k])],
                                 sem="ds2%d" % k)
                pd = main[:, 2048 + k * 512: 2048 + (k + 1) * 512]
                for a in range(24):
                    w = [("dht%d" % tb, ht_ds[tb]), da_wd,
                         ("ac", dps_ac[k])] if a == 0 else []
                    if a == 0 and cp == 0 and tc == 0:
                        w.append(("dv", last_ymul_dv))
                    if a == 0 and cp == 0:
                        w.append(("dpt0", dpt_mark[0]))
                        w.append(("dpt1", dpt_mark[1]))
                    st_ap = s_wdb[:, a * 1024 + cp * 128: a * 1024 + (cp + 1) * 128]
                    rhs = s_ht[:, tb * 12288 + a * 512: tb * 12288 + (a + 1) * 512]
                    s.add("tensor", lambda q, ps=pd, a_=st_ap, r=rhs, a2=a:
                          q.matmul(ps, a_, r, start=(a2 == 0), stop=(a2 == 23)),
                          inc=[("pe", 1)], wait=w)
                pe_d = s.cnt["pe"]
                ht_pe[tb] = pe_d
                gst = s_gstg[:, k * 512:(k + 1) * 512]
                s.add("scalar", lambda q, o=gst, i_=pd: q.activation(
                    o, i_, AF.Silu, scale=S127),
                    inc=[("ac", 1)], wait=[("pe", pe_d), ("dv", gstg_dv[k])])
                dps_ac[k] = s.cnt["ac"]
                ptl = s_pt[:, k * 512:(k + 1) * 512]
                s.add("vector", lambda q, o=ptl, s_=s_s2[:, k * 512:(k + 1) * 512],
                      g_=gst, b=s_bproj[:, cp:cp + 1]:
                      q.scalar_tensor_tensor(o, s_, b, g_, ALU.add, ALU.mult),
                      inc=[("dv", 1)],
                      wait=[("ac", dps_ac[k]), ("ds2%d" % k, s2_ds[k]), p_dv[k]])
                gstg_dv[k] = s.cnt["dv"]
                s2_dv[k] = s.cnt["dv"]
                dv_p = s.cnt["dv"]
                s.dma("sync", p_dram.ap()[cp * 128:(cp + 1) * 128,
                                          tc * 512:(tc + 1) * 512],
                      ptl, wait=[("dv", dv_p)], sem=psem)
                p_dv[k] = (psem, s.cnt[psem])
            tc_p_ds = s.cnt[psem]
            for i in range(4):
                tt = tc * 4 + i
                kk = tt % 2
                tsem = "dpt%d" % kk
                s.dmaT("scalar", s_ptT[:, kk * 1024:(kk + 1) * 1024],
                       p_dram.ap()[:, tt * 128:(tt + 1) * 128],
                       wait=[(psem, tc_p_ds), ("dv", out_dv[kk])], sem=tsem)
                da_p = s.cnt[tsem]
                dpt_mark[kk] = da_p
                ds_x = s.dma("sync", s_xo[:, kk * 1024:(kk + 1) * 1024],
                             x_own.ap()[tt * 128:(tt + 1) * 128, :],
                             wait=[("dv", out_dv[kk])], sem="dxo%d" % kk)
                s.add("vector", lambda q, o=s_outt[:, kk * 1024:(kk + 1) * 1024],
                      a=s_xo[:, kk * 1024:(kk + 1) * 1024],
                      b=s_ptT[:, kk * 1024:(kk + 1) * 1024]:
                      q.tensor_add(o, a, b),
                      inc=[("dv", 1)],
                      wait=[(tsem, da_p), ("dxo%d" % kk, ds_x),
                            ("dout%d" % kk, out_st[kk])])
                out_dv[kk] = s.cnt["dv"]
                out_st[kk] = s.dma("sync", out_own.ap()[tt * 128:(tt + 1) * 128, :],
                                   s_outt[:, kk * 1024:(kk + 1) * 1024],
                                   wait=[("dv", out_dv[kk])], sem="dout%d" % kk)
        final_waits = [(n, s.cnt[n]) for n in list(s.cnt.keys())
                       if n not in ("pe", "dv", "ac", "cc") and s.cnt[n] > 0]
        s.add("sync", lambda q: q.nop(), wait=final_waits)
        s.emit(block)
    return nc


_NC_CACHE = {}


def kernel(L0_attn_x, L0_attn_W_attn, L0_attn_b_attn, L0_attn_W_proj, L0_attn_b_proj,
           L0_ffn_w_gate, L0_ffn_w_up, L0_ffn_w_down):
    x = np.asarray(L0_attn_x)
    W_attn = np.asarray(L0_attn_W_attn)
    b_attn = np.asarray(L0_attn_b_attn)
    W_proj = np.asarray(L0_attn_W_proj)
    b_proj = np.asarray(L0_attn_b_proj)
    wgT = np.ascontiguousarray(np.asarray(L0_ffn_w_gate).T)
    wuT = np.ascontiguousarray(np.asarray(L0_ffn_w_up).T)
    wdT = np.ascontiguousarray(np.asarray(L0_ffn_w_down).T)

    identc = np.eye(128, dtype=np.float32).astype(ml_dtypes.bfloat16)
    mw = np.zeros((128, 896), dtype=np.float32)
    r = np.arange(128)[:, None]
    cc = np.arange(896)[None, :]
    mw[(r <= cc - 384)] = 1.0
    maskwc = mw.astype(ml_dtypes.bfloat16)
    onesc = np.ones((1, 128), dtype=np.float32)
    bproj_t = np.ascontiguousarray(b_proj.reshape(8, 128).T.astype(np.float32))

    if "nc" not in _NC_CACHE:
        _NC_CACHE["nc"] = build_nc()
    nc = _NC_CACHE["nc"]

    in_maps = []
    for core in range(8):
        b, j = core // 4, core % 4
        qs = W_attn[:, 256 * j:256 * (j + 1)]
        ks = W_attn[:, 1024 + 256 * j:1024 + 256 * (j + 1)]
        vs = W_attn[:, 2048 + 256 * j:2048 + 256 * (j + 1)]
        wqkv = np.ascontiguousarray(np.concatenate([qs, ks, vs], axis=1))
        bq = np.concatenate([b_attn[256 * j:256 * (j + 1)],
                             b_attn[1024 + 256 * j:1024 + 256 * (j + 1)],
                             b_attn[2048 + 256 * j:2048 + 256 * (j + 1)]])
        bqkv_t = np.ascontiguousarray(bq.reshape(6, 128).T.astype(np.float32))
        in_maps.append({
            "x_full": np.ascontiguousarray(x[b]),
            "x_own": np.ascontiguousarray(x[b, j * TS:(j + 1) * TS]),
            "wqkv": wqkv,
            "bqkv": bqkv_t,
            "wproj": np.ascontiguousarray(W_proj[256 * j:256 * (j + 1), :]),
            "bproj": bproj_t,
            "wgT": wgT, "wuT": wuT, "wdT": wdT,
            "ident": identc, "maskw": maskwc, "ones1": onesc,
        })
    res = run_bass_kernel_spmd(nc, in_maps, core_ids=list(range(8)))
    out = np.empty((2, T, C), dtype=np.float32)
    for core in range(8):
        b, j = core // 4, core % 4
        out[b, j * TS:(j + 1) * TS] = res.results[core]["out_own"]
    return out
